# version 02
# baseline (speedup 1.0000x reference)
import numpy as np
import ml_dtypes
from contextlib import ExitStack

import concourse.bass as bass
import concourse.mybir as mybir
from concourse.bass_utils import run_bass_kernel_spmd

F32 = mybir.dt.float32
F16 = mybir.dt.float16
BF16 = mybir.dt.bfloat16
U16 = mybir.dt.uint16
AF = mybir.ActivationFunctionType
ALU = mybir.AluOpType
AXL = mybir.AxisListType

R = 8
TT = 16384
D = 4096
TCORE = 2048
DC = 512
NST = 16
DTR = 128
ITR = 11008
NKT = 32
NIT = 86
TG = 1024
TC = 512
NCH = TT // TC
EPS = 1e-5

ENGINES = ("sync", "vector", "scalar", "tensor", "gpsimd")


class Prog:
    def __init__(self):
        self.steps = []
        self.counts = {}

    def ev(self, sem):
        return (sem, self.counts.get(sem, 0))

    def emit(self, eng, fn, waits=(), incs=()):
        out = []
        for s, n in incs:
            self.counts[s] = self.counts.get(s, 0) + n
            out.append((s, self.counts[s]))
        self.steps.append((eng, fn, tuple(waits)))
        return out

    def barrier(self):
        for e in ENGINES:
            self.emit(e, lambda eng, S: eng.nop().then_inc(S["bar"], 1),
                      incs=[("bar", 1)])
        tgt = self.counts["bar"]
        for e in ENGINES:
            self.emit(e, None, waits=[("bar", tgt)])


def build_program():
    nc = bass.Bass("TRN2", target_bir_lowering=False, debug=False)

    def ein(name, shape, dt):
        return nc.dram_tensor(name, list(shape), dt, kind="ExternalInput").ap()

    xr = ein("xr", (TCORE, D), F32)
    xd = ein("xd", (TT, DC), F32)
    ah = ein("ah", (128, 64), F32)
    dtw = ein("dtw", (DTR, DC), F32)
    dtb = ein("dtb", (128, 4), F32)
    dvec = ein("dvec", (128, 4), F32)
    xpw = ein("xpw", (D, 160), F32)
    qg = ein("qg", (NIT, 128, NKT, 128), U16)
    qu = ein("qu", (NIT, 128, NKT, 128), U16)
    sg = ein("sg", (NIT, 128, NKT, 128), F16)
    su = ein("su", (NIT, 128, NKT, 128), F16)
    qd = ein("qd", (ITR, D), U16)
    sd = ein("sd", (ITR, D), F16)
    zg = ein("zg", (32, 2752), U16)
    zu = ein("zu", (32, 2752), U16)
    zd = ein("zd", (NIT, 1024), U16)
    sgr = ein("sgr", (32, ITR), F16)
    sur = ein("sur", (32, ITR), F16)
    sdr = ein("sdr", (NIT, D), F16)
    msk = ein("msk", (128, R), F32)

    o = nc.dram_tensor("o", [TCORE, D], F32, kind="ExternalOutput").ap()

    ag_in = nc.dram_tensor("ag_in", [162, TCORE], F32, kind="Internal").ap()
    ag_out = nc.dram_tensor("ag_out", [162 * R, TCORE], F32, kind="Internal").ap()
    xnT_dram = nc.dram_tensor("xnT_dram", [NKT, 128, TG], BF16, kind="Internal").ap()
    zin = nc.dram_tensor("zin", [TT, D], F32, kind="Internal").ap()
    sout = nc.dram_tensor("sout", [TCORE, D], F32, kind="Internal").ap()
    inter_dram = nc.dram_tensor("inter_dram", [NIT, 128, TCORE], BF16, kind="Internal").ap()
    wd_deq = nc.dram_tensor("wd_deq", [NIT, NKT, 128, 128], BF16, kind="Internal").ap()
    zp_dram = nc.dram_tensor("zp_dram", [3, 128, ITR], BF16, kind="Internal").ap()

    id_bf = nc.inline_tensor(np.eye(128).astype(ml_dtypes.bfloat16), "c_id_bf").ap()
    id_f32 = nc.inline_tensor(np.eye(128).astype(np.float32), "c_id_f32").ap()
    id_f16 = nc.inline_tensor(np.eye(128).astype(np.float16), "c_id_f16").ap()
    shiftv = nc.inline_tensor(
        (4 * (np.arange(128) % 4)).astype(np.uint16).reshape(128, 1), "c_shiftv").ap()
    sh4 = nc.inline_tensor(
        np.tile(np.array([0, 4, 8, 12], np.uint16), (128, 1)), "c_sh4").ap()
    indg_np = np.zeros((128, 32 * 32), np.float32)
    for g in range(32):
        indg_np[:, g * 32 + g] = 1.0
    indg = nc.inline_tensor(indg_np.astype(ml_dtypes.bfloat16), "c_indg").ap()
    indd_np = np.zeros((128, NIT * NIT), np.float32)
    for g in range(NIT):
        indd_np[:, g * NIT + g] = 1.0
    indd = nc.inline_tensor(indd_np.astype(ml_dtypes.bfloat16), "c_indd").ap()

    P = Prog()
    sb = {}
    ps = {}
    guards = {}

    def alloc(name, shape, dt):
        g = nc.sbuf_tensor("sb_" + name, list(shape), dt)
        sb[name] = g.__enter__()
        guards[name] = g
        return sb[name]

    def sfree(*names):
        for n in names:
            guards.pop(n).__exit__(None, None, None)

    def palloc(name, shape, dt=F32):
        g = nc.psum_tensor("ps_" + name, list(shape), dt)
        ps[name] = g.__enter__()
        guards["#p_" + name] = g
        return ps[name]

    def pfree(*names):
        for n in names:
            guards.pop("#p_" + n).__exit__(None, None, None)

    alloc("id_bf", (128, 128), BF16)
    alloc("id_f32", (128, 128), F32)
    alloc("id_f16", (128, 128), F16)
    alloc("shiftv", (128, 1), U16)
    alloc("sh4", (128, 4), U16)
    alloc("A", (128, 64), F32)
    alloc("dtwb", (128, DC), BF16)
    alloc("dtb", (128, 4), F32)
    alloc("dvec", (128, 4), F32)
    alloc("xpwb", (128, NKT * 160), BF16)
    alloc("indg", (128, 32 * 32), BF16)
    alloc("msk", (128, R), F32)
    alloc("xg", (32, TCORE), BF16)
    alloc("xgi", (NIT, TCORE), BF16)
    alloc("indd", (128, NIT * NIT), BF16)
    alloc("xnT", (128, NKT * TG), BF16)

    def dma_to(dst, src, q="lq", eng=None):
        def fn(e, S, dst=dst, src=src, q=q):
            return e.dma_start(out=dst, in_=src).then_inc(S[q], 16)
        return fn

    LD = "sync"
    STQ = "scalar"

    for dst, src in [
        (sb["id_bf"][:], id_bf), (sb["id_f32"][:], id_f32),
        (sb["id_f16"][:], id_f16), (sb["shiftv"][:], shiftv),
        (sb["sh4"][:], sh4), (sb["dtb"][:], dtb), (sb["dvec"][:], dvec),
        (sb["indg"][:], indg), (sb["indd"][:], indd),
        (sb["msk"][:], msk),
    ]:
        P.emit(LD, dma_to(dst, src), incs=[("lq", 16)])

    alloc("c_ah", (128, 64), F32)
    alloc("c_dtw", (128, DC), F32)
    alloc("c_xpw", (128, NKT * 160), F32)
    P.emit(LD, dma_to(sb["c_ah"][:], ah), incs=[("lq", 16)])
    P.emit(LD, dma_to(sb["c_dtw"][:], dtw), incs=[("lq", 16)])
    P.emit(LD, dma_to(sb["c_xpw"][:].rearrange("p (k r) -> p k r", k=NKT),
                      xpw.rearrange("(k p) r -> p k r", p=128)), incs=[("lq", 16)])

    alloc("z_in", (NIT, 2752), U16)
    alloc("z_sh", (NIT, ITR), U16)
    alloc("z_s", (NIT, ITR), F16)
    alloc("z_out", (NIT, ITR), BF16)

    def z_extract(zsrc, grows, gcols, s_src, zp_row):
        nn = gcols * 4
        evl = P.emit(LD, dma_to(sb["z_in"][0:grows, 0:gcols], zsrc),
                     waits=[("v", P.ev("v")[1])], incs=[("lq", 16)])
        evl2 = P.emit(LD, dma_to(sb["z_s"][0:grows, 0:nn], s_src), incs=[("lq", 16)])

        def f_shift(eng, S, grows=grows, gcols=gcols, nn=nn):
            last = None
            for j in range(4):
                ov = sb["z_sh"][0:grows, 0:nn].rearrange("g (m j) -> g m j", j=4)[:, :, j]
                last = eng.tensor_scalar(out=ov, in0=sb["z_in"][0:grows, 0:gcols],
                                         scalar1=4 * j, scalar2=15,
                                         op0=ALU.logical_shift_right, op1=ALU.bitwise_and)
            return last.then_inc(S["v"], 1)
        ev1 = P.emit("vector", f_shift,
                     waits=[("lq", evl2[0][1]), ("v", P.ev("v")[1])], incs=[("v", 1)])

        def f_p1(eng, S, grows=grows, nn=nn):
            return eng.tensor_single_scalar(sb["z_sh"][0:grows, 0:nn],
                                            sb["z_sh"][0:grows, 0:nn],
                                            1, ALU.add).then_inc(S["v"], 1)
        ev2 = P.emit("vector", f_p1, waits=[("v", ev1[0][1])], incs=[("v", 1)])

        def f_zmul(eng, S, grows=grows, nn=nn):
            return eng.tensor_mul(sb["z_out"][0:grows, 0:nn], sb["z_sh"][0:grows, 0:nn],
                                  sb["z_s"][0:grows, 0:nn]).then_inc(S["v"], 1)
        evz = P.emit("vector", f_zmul,
                     waits=[("v", ev2[0][1]), ("lq", evl2[0][1])], incs=[("v", 1)])
        P.emit(STQ, dma_to(zp_dram[zp_row, 0:grows, 0:nn],
                           sb["z_out"][0:grows, 0:nn], q="sq"),
               waits=[("v", evz[0][1])], incs=[("sq", 16)])

    z_extract(zg, 32, 2752, sgr, 0)
    z_extract(zu, 32, 2752, sur, 1)
    z_extract(zd, NIT, 1024, sdr, 2)

    ev_cl = P.ev("lq")

    def f_A(eng, S):
        eng.wait_ge(S["lq"], ev_cl[1])
        eng.activation(out=sb["A"][:], in_=sb["c_ah"][:], func=AF.Exp)
        return eng.activation(out=sb["A"][:], in_=sb["A"][:], func=AF.Copy,
                              scale=-1.0).then_inc(S["a"], 1)
    P.emit("scalar", f_A, incs=[("a", 1)])

    def f_casts(eng, S):
        eng.wait_ge(S["lq"], ev_cl[1])
        eng.tensor_copy(sb["dtwb"][:], sb["c_dtw"][:])
        return eng.tensor_copy(sb["xpwb"][:], sb["c_xpw"][:]).then_inc(S["v"], 1)
    P.emit("vector", f_casts, incs=[("v", 1)])

    P.emit(STQ, None, waits=[("sq", P.ev("sq")[1])])
    P.barrier()
    sfree("z_out", "z_s", "z_sh", "z_in", "c_xpw", "c_dtw", "c_ah")

    alloc("xt", (128, D), F32)
    alloc("xsq", (128, D), F32)
    alloc("xn", (128, D), BF16)
    alloc("cols", (128, 8), F32)
    alloc("xpst", (128, TG), F32)

    palloc("p_tr", (128, 512), BF16)
    palloc("p_xp", (128, TG), F32)
    palloc("p_bc", (32, TG), F32)
    palloc("p_xg", (32, TG), F32)

    for h in (1, 0):
        for tl in range(TG // 128):
            tt = h * (TG // 128) + tl
            evx = P.emit(LD, dma_to(sb["xt"][:], xr[tt * 128:(tt + 1) * 128, :]),
                         waits=[("v", P.ev("v")[1]), ("a", P.ev("a")[1])],
                         incs=[("lq", 16)])

            def f_stats_a(eng, S):
                c = sb["cols"]
                eng.tensor_mul(sb["xsq"][:], sb["xt"][:], sb["xt"][:])
                eng.reduce_sum(c[:, 0:1], sb["xt"][:], axis=AXL.X)
                return eng.reduce_sum(c[:, 1:2], sb["xsq"][:], axis=AXL.X).then_inc(S["v"], 1)
            evsa = P.emit("vector", f_stats_a, waits=[("lq", evx[0][1])],
                          incs=[("v", 1)])

            def f_stats_s(eng, S):
                c = sb["cols"]
                eng.mul(c[:, 2:3], c[:, 0:1], 1.0 / D)
                eng.activation(out=c[:, 3:4], in_=c[:, 1:2], func=AF.Copy,
                               scale=1.0 / D, bias=EPS)
                eng.activation(out=c[:, 7:8], in_=c[:, 2:3], func=AF.Square)
                eng.activation(out=c[:, 5:6], in_=c[:, 7:8], func=AF.Ln,
                               scale=-1.0, bias=c[:, 3:4])
                eng.mul(c[:, 5:6], c[:, 5:6], -0.5)
                return eng.activation(out=c[:, 6:7], in_=c[:, 5:6],
                                      func=AF.Exp).then_inc(S["a"], 1)
            evr = P.emit("scalar", f_stats_s,
                         waits=[("v", evsa[0][1]), ("sq", P.ev("sq")[1])],
                         incs=[("a", 1)])

            def f_xn(eng, S):
                c = sb["cols"]
                return eng.tensor_scalar(out=sb["xn"][:], in0=sb["xt"][:],
                                         scalar1=c[:, 2:3], scalar2=c[:, 6:7],
                                         op0=ALU.subtract, op1=ALU.mult).then_inc(S["v"], 1)
            evn = P.emit("vector", f_xn,
                         waits=[("a", evr[0][1]), ("t", P.ev("t")[1])],
                         incs=[("v", 1)])

            def f_stst(eng, S, tt=tt):
                c = sb["cols"]
                eng.dma_start(out=ag_in[160:161, tt * 128:(tt + 1) * 128],
                              in_=c[:, 2:3]).then_inc(S["sq"], 16)
                return eng.dma_start(out=ag_in[161:162, tt * 128:(tt + 1) * 128],
                                     in_=c[:, 6:7]).then_inc(S["sq"], 16)
            P.emit(STQ, f_stst, waits=[("a", evr[0][1])], incs=[("sq", 32)])

            for j in range(8):
                def f_tp(eng, S, j=j):
                    last = None
                    for q in range(4):
                        k = j * 4 + q
                        last = eng.transpose(ps["p_tr"][:, q * 128:(q + 1) * 128],
                                             sb["xn"][:, k * 128:(k + 1) * 128],
                                             sb["id_bf"][:])
                    return last.then_inc(S["t"], 1)
                evt = P.emit("tensor", f_tp,
                             waits=[("v", evn[0][1]),
                                    ("v", P.ev("v")[1])],
                             incs=[("t", 1)])

                def f_ev(eng, S, j=j, tl=tl):
                    dst = sb["xnT"][:].rearrange("p (k t) -> p k t", k=NKT)[
                        :, j * 4:(j + 1) * 4, tl * 128:(tl + 1) * 128]
                    src = ps["p_tr"][:].rearrange("p (q t) -> p q t", q=4)
                    return eng.tensor_copy(dst, src).then_inc(S["v"], 1)
                evtc = P.emit("vector", f_ev, waits=[("t", evt[0][1])],
                              incs=[("v", 1)])
                P.emit("tensor", None, waits=[("v", evtc[0][1])])

        def f_xp(eng, S):
            last = None
            for k in range(NKT):
                st = (k == 0)
                sp = (k == NKT - 1)
                for hh in range(2):
                    rr = slice(hh * 512, (hh + 1) * 512)
                    eng.matmul(ps["p_xp"][:, rr],
                               sb["xpwb"][:, k * 160:k * 160 + 128],
                               sb["xnT"][:, k * TG:k * TG + TG][:, rr],
                               start=st, stop=sp)
                    eng.matmul(ps["p_bc"][:, rr],
                               sb["xpwb"][:, k * 160 + 128:(k + 1) * 160],
                               sb["xnT"][:, k * TG:k * TG + TG][:, rr],
                               start=st, stop=sp)
                    last = eng.matmul(ps["p_xg"][:, rr],
                                      sb["indg"][:, k * 32:(k + 1) * 32],
                                      sb["xnT"][:, k * TG:k * TG + TG][:, rr],
                                      start=st, stop=sp)
            return last.then_inc(S["t"], 1)
        evm = P.emit("tensor", f_xp, incs=[("t", 1)])

        def f_xpev(eng, S, h=h):
            eng.tensor_copy(sb["xpst"][:], ps["p_xp"][:])
            return eng.tensor_scalar(out=sb["xg"][0:32, h * TG:(h + 1) * TG],
                                     in0=ps["p_xg"][:], scalar1=-1.0,
                                     scalar2=None, op0=ALU.mult).then_inc(S["v"], 1)
        evxp = P.emit("vector", f_xpev, waits=[("t", evm[0][1])], incs=[("v", 1)])

        def f_bcev(eng, S):
            return eng.tensor_copy(sb["xpst"][0:32, 0:TG], ps["p_bc"][:]).then_inc(S["v"], 1)
        def f_st1(eng, S, h=h):
            return eng.dma_start(out=ag_in[0:128, h * TG:(h + 1) * TG],
                                 in_=sb["xpst"][:]).then_inc(S["sq"], 16)
        ev1 = P.emit(STQ, f_st1, waits=[("v", evxp[0][1])], incs=[("sq", 16)])
        evbc = P.emit("vector", f_bcev, waits=[("t", evm[0][1]), ("sq", ev1[0][1])],
                      incs=[("v", 1)])

        def f_st2(eng, S, h=h):
            return eng.dma_start(out=ag_in[128:160, h * TG:(h + 1) * TG],
                                 in_=sb["xpst"][0:32, 0:TG]).then_inc(S["sq"], 16)
        ev2 = P.emit(STQ, f_st2, waits=[("v", evbc[0][1])], incs=[("sq", 16)])
        P.emit("tensor", None, waits=[("sq", ev2[0][1])])

        if h == 1:
            def f_xstash(eng, S):
                return eng.dma_start(
                    out=xnT_dram.rearrange("k p t -> p k t"),
                    in_=sb["xnT"][:].rearrange("p (k t) -> p k t", k=NKT),
                ).then_inc(S["sq"], 16)
            evst = P.emit(STQ, f_xstash, incs=[("sq", 16)])
            P.emit("vector", None, waits=[("sq", evst[0][1])])
            P.emit("tensor", None, waits=[("sq", evst[0][1])])

    ev_agin = P.ev("sq")
    P.barrier()
    sfree("xpst", "cols", "xn", "xsq", "xt")
    pfree("p_xg", "p_bc", "p_xp", "p_tr")

    def f_ag(eng, S):
        return eng.collective_compute(
            "AllGather", ALU.bypass, replica_groups=[list(range(R))],
            ins=[ag_in.tensor.ap().opt()], outs=[ag_out.tensor.ap().opt()],
        ).then_inc(S["c"], 1)
    ev_ag = P.emit("gpsimd", f_ag, waits=[("sq", ev_agin[1])], incs=[("c", 1)])

    alloc("dtin", (128, TC), F32)
    alloc("dtinb", (128, TC), BF16)
    alloc("dtc", (128, 4 * TC), F32)
    alloc("spA", (128, TC), F32)
    alloc("spB", (128, TC), F32)
    alloc("xnd", (128, 4 * TC), F32)
    alloc("uu", (128, 4 * TC), F32)
    alloc("mrep", (128, TC), F32)
    alloc("rrep", (128, TC), F32)
    alloc("brep", (128, 2 * TC), F32)
    alloc("crep", (128, 2 * TC), F32)
    alloc("dab", (128, 2 * TC), F32)
    alloc("dbxb", (128, 2 * TC), F32)
    alloc("hb", (128, 2 * TC), F32)
    alloc("yac", (128, 4 * TC), F32)
    alloc("state", (128, 64), F32)
    alloc("mst", (128, 2 * TC), F32)

    palloc("p_dt0", (128, TC))
    palloc("p_dt1", (128, TC))
    palloc("p_dt2", (128, TC))
    palloc("p_dt3", (128, TC))
    palloc("p_xtr", (128, TC))
    palloc("p_str", (128, TC))

    P.emit("vector", None, waits=[("c", ev_ag[0][1])])
    P.emit("scalar", None, waits=[("c", ev_ag[0][1])])
    P.emit("tensor", None, waits=[("c", ev_ag[0][1])])
    P.emit(LD, None, waits=[("c", ev_ag[0][1])])

    for ch in range(NCH):
        blk = ch // 4
        lt = (ch % 4) * TC
        base = blk * 162

        evd = P.emit(LD, dma_to(sb["dtin"][:],
                                ag_out[base:base + 128, lt:lt + TC]),
                     waits=[("t", P.ev("t")[1]), ("v", P.ev("v")[1])],
                     incs=[("lq", 16)])
        evmr = P.emit(LD, dma_to(
            sb["mrep"][:], ag_out[base + 160:base + 161, lt:lt + TC].partition_broadcast(128)),
            incs=[("lq", 16)])
        evrr = P.emit(LD, dma_to(
            sb["rrep"][:], ag_out[base + 161:base + 162, lt:lt + TC].partition_broadcast(128)),
            incs=[("lq", 16)])

        evcast = P.emit("vector",
                        lambda eng, S: eng.tensor_copy(sb["dtinb"][:], sb["dtin"][:]).then_inc(S["v"], 1),
                        waits=[("lq", evrr[0][1])], incs=[("v", 1)])

        def f_dtmm(eng, S):
            last = None
            for i in range(4):
                last = eng.matmul(ps[f"p_dt{i}"][:], sb["dtwb"][:, i * 128:(i + 1) * 128],
                                  sb["dtinb"][:], start=True, stop=True)
            return last.then_inc(S["t"], 1)
        evdt = P.emit("tensor", f_dtmm,
                      waits=[("v", evcast[0][1]), ("a", P.ev("a")[1]),
                             ("v", P.ev("v")[1])],
                      incs=[("t", 1)])

        for i in range(4):
            def f_sp_a(eng, S, i=i):
                eng.activation(out=sb["spA"][:], in_=ps[f"p_dt{i}"][:], func=AF.Abs,
                               bias=sb["dtb"][:, i:i + 1])
                eng.activation(out=sb["spB"][:], in_=sb["spA"][:], func=AF.Exp, scale=-1.0)
                return eng.activation(out=sb["spA"][:], in_=sb["spB"][:], func=AF.Ln,
                                      bias=1.0).then_inc(S["a"], 1)
            eva = P.emit("scalar", f_sp_a,
                         waits=[("t", evdt[0][1]), ("v", P.ev("v")[1])],
                         incs=[("a", 1)])

            def f_sp_v(eng, S, i=i):
                eng.tensor_scalar(out=sb["dtc"][:, i * TC:(i + 1) * TC],
                                  in0=ps[f"p_dt{i}"][:], scalar1=sb["dtb"][:, i:i + 1],
                                  scalar2=0.0, op0=ALU.add, op1=ALU.max)
                return eng.tensor_add(sb["dtc"][:, i * TC:(i + 1) * TC],
                                      sb["dtc"][:, i * TC:(i + 1) * TC],
                                      sb["spA"][:]).then_inc(S["v"], 1)
            evv = P.emit("vector", f_sp_v, waits=[("a", eva[0][1])], incs=[("v", 1)])
        ev_dtdone = P.ev("v")

        for s4 in range(4):
            def f_xl(eng, S, s4=s4, ch=ch):
                return eng.dma_start(
                    out=sb["mst"][:, 0:DC],
                    in_=xd[ch * TC + s4 * 128: ch * TC + (s4 + 1) * 128, :],
                ).then_inc(S["lq"], 16)
            evxl = P.emit(LD, f_xl, waits=[("t", P.ev("t")[1])], incs=[("lq", 16)])

            def f_xtr(eng, S, s4=s4):
                last = None
                for i in range(4):
                    last = eng.transpose(ps["p_xtr"][:, i * 128:(i + 1) * 128],
                                         sb["mst"][:, i * 128:(i + 1) * 128],
                                         sb["id_f32"][:])
                return last.then_inc(S["t"], 1)
            evxt = P.emit("tensor", f_xtr, waits=[("lq", evxl[0][1])], incs=[("t", 1)])

            def f_xev(eng, S, s4=s4):
                dst = sb["xnd"][:].rearrange("p (i t) -> p i t", i=4)[
                    :, :, s4 * 128:(s4 + 1) * 128]
                src = ps["p_xtr"][:].rearrange("p (i t) -> p i t", i=4)
                mr = sb["mrep"][:, s4 * 128:(s4 + 1) * 128].unsqueeze(1).broadcast_to([128, 4, 128])
                rr2 = sb["rrep"][:, s4 * 128:(s4 + 1) * 128].unsqueeze(1).broadcast_to([128, 4, 128])
                eng.tensor_sub(dst, src, mr)
                return eng.tensor_mul(dst, dst, rr2).then_inc(S["v"], 1)
            evxe = P.emit("vector", f_xev,
                          waits=[("t", evxt[0][1]), ("lq", evmr[0][1]),
                                 ("lq", evrr[0][1])], incs=[("v", 1)])
            P.emit("tensor", None, waits=[("v", evxe[0][1])])
        ev_xnd = P.ev("v")

        def f_u(eng, S):
            return eng.tensor_mul(sb["uu"][:], sb["dtc"][:], sb["xnd"][:]).then_inc(S["v"], 1)
        ev_u = P.emit("vector", f_u, waits=[("v", ev_xnd[1]), ("v", ev_dtdone[1])],
                      incs=[("v", 1)])

        for npair in range(8):
            n0 = 2 * npair
            def f_bc2(eng, S, n0=n0, base=base, lt=lt):
                eng.dma_start(out=sb["brep"][:, 0:TC],
                              in_=ag_out[base + 128 + n0:base + 129 + n0, lt:lt + TC]
                              .partition_broadcast(128)).then_inc(S["lq"], 16)
                eng.dma_start(out=sb["brep"][:, TC:2 * TC],
                              in_=ag_out[base + 129 + n0:base + 130 + n0, lt:lt + TC]
                              .partition_broadcast(128)).then_inc(S["lq"], 16)
                eng.dma_start(out=sb["crep"][:, 0:TC],
                              in_=ag_out[base + 144 + n0:base + 145 + n0, lt:lt + TC]
                              .partition_broadcast(128)).then_inc(S["lq"], 16)
                return eng.dma_start(out=sb["crep"][:, TC:2 * TC],
                                     in_=ag_out[base + 145 + n0:base + 146 + n0, lt:lt + TC]
                                     .partition_broadcast(128)).then_inc(S["lq"], 16)
            evb = P.emit(LD, f_bc2,
                         waits=[("v", P.ev("v")[1]), ("p", P.ev("p")[1])],
                         incs=[("lq", 64)])
            evc = evb

            for i in range(4):
                def f_dA(eng, S, i=i, n0=n0):
                    eng.activation(out=sb["dab"][:, 0:TC], in_=sb["dtc"][:, i * TC:(i + 1) * TC],
                                   func=AF.Exp, scale=sb["A"][:, i * 16 + n0:i * 16 + n0 + 1])
                    return eng.activation(out=sb["dab"][:, TC:2 * TC],
                                          in_=sb["dtc"][:, i * TC:(i + 1) * TC],
                                          func=AF.Exp,
                                          scale=sb["A"][:, i * 16 + n0 + 1:i * 16 + n0 + 2]
                                          ).then_inc(S["a"], 1)
                eva2 = P.emit("scalar", f_dA,
                              waits=[("v", ev_dtdone[1]), ("p", P.ev("p")[1]),
                                     ("v", P.ev("v")[1])],
                              incs=[("a", 1)])

                def f_dbx(eng, S, i=i):
                    eng.tensor_mul(sb["dbxb"][:, 0:TC], sb["uu"][:, i * TC:(i + 1) * TC],
                                   sb["brep"][:, 0:TC])
                    return eng.tensor_mul(sb["dbxb"][:, TC:2 * TC],
                                          sb["uu"][:, i * TC:(i + 1) * TC],
                                          sb["brep"][:, TC:2 * TC]).then_inc(S["v"], 1)
                evbx = P.emit("vector", f_dbx,
                              waits=[("v", ev_u[0][1]), ("lq", evb[0][1])],
                              incs=[("v", 1)])

                def f_tts(eng, S, i=i, n0=n0, ch=ch):
                    col = i * 16 + n0
                    if ch % 16 == 0:
                        ini0 = 0.0
                        ini1 = 0.0
                    else:
                        ini0 = sb["state"][:, col:col + 1]
                        ini1 = sb["state"][:, col + 1:col + 2]
                    eng.tensor_tensor_scan(out=sb["hb"][:, 0:TC], data0=sb["dab"][:, 0:TC],
                                           data1=sb["dbxb"][:, 0:TC], initial=ini0,
                                           op0=ALU.mult, op1=ALU.add)
                    return eng.tensor_tensor_scan(out=sb["hb"][:, TC:2 * TC],
                                                  data0=sb["dab"][:, TC:2 * TC],
                                                  data1=sb["dbxb"][:, TC:2 * TC], initial=ini1,
                                                  op0=ALU.mult, op1=ALU.add).then_inc(S["v"], 1)
                evts0 = P.emit("vector", f_tts,
                               waits=[("a", eva2[0][1]), ("v", evbx[0][1])],
                               incs=[("v", 1)])

                def f_stcp(eng, S, i=i, n0=n0):
                    col = i * 16 + n0
                    hsrc = sb["hb"][:].rearrange("p (nn t) -> p nn t", nn=2)[:, :, TC - 1:TC]
                    return eng.tensor_copy(sb["state"][:, col:col + 2].unsqueeze(2),
                                           hsrc).then_inc(S["v"], 1)
                evts = P.emit("vector", f_stcp, waits=[("v", evts0[0][1])],
                              incs=[("v", 1)])

                def f_cm(eng, S, i=i):
                    return eng.tensor_mul(sb["dab"][:], sb["hb"][:], sb["crep"][:]
                                          ).then_inc(S["p"], 1)
                evcm = P.emit("gpsimd", f_cm,
                              waits=[("v", evts[0][1]), ("lq", evb[0][1]),
                                     ("a", eva2[0][1])],
                              incs=[("p", 1)])

                def f_ya(eng, S, i=i, npair=npair):
                    ysl = sb["yac"][:, i * TC:(i + 1) * TC]
                    if npair == 0:
                        last = eng.tensor_add(ysl, sb["dab"][:, 0:TC], sb["dab"][:, TC:2 * TC])
                    else:
                        eng.tensor_add(ysl, ysl, sb["dab"][:, 0:TC])
                        last = eng.tensor_add(ysl, ysl, sb["dab"][:, TC:2 * TC])
                    return last.then_inc(S["v"], 1)
                P.emit("vector", f_ya, waits=[("p", evcm[0][1])], incs=[("v", 1)])

        def f_dterm(eng, S):
            last = None
            for i in range(4):
                last = eng.scalar_tensor_tensor(
                    out=sb["yac"][:, i * TC:(i + 1) * TC],
                    in0=sb["xnd"][:, i * TC:(i + 1) * TC],
                    scalar=sb["dvec"][:, i:i + 1],
                    in1=sb["yac"][:, i * TC:(i + 1) * TC],
                    op0=ALU.mult, op1=ALU.add)
            return last.then_inc(S["v"], 1)
        evdterm = P.emit("vector", f_dterm, incs=[("v", 1)])

        for s4 in range(4):
            def f_str(eng, S, s4=s4):
                last = None
                for i in range(4):
                    last = eng.transpose(
                        ps["p_str"][:, i * 128:(i + 1) * 128],
                        sb["yac"][:, i * TC + s4 * 128:i * TC + (s4 + 1) * 128],
                        sb["id_f32"][:])
                return last.then_inc(S["t"], 1)
            evst2 = P.emit("tensor", f_str, waits=[("v", evdterm[0][1])], incs=[("t", 1)])

            for v in range(R):
                sl = v % 2
                def f_mev(eng, S, v=v, sl=sl):
                    return eng.mul(sb["mst"][:, sl * TC:(sl + 1) * TC],
                                   ps["p_str"][:],
                                   sb["msk"][:, v:v + 1]).then_inc(S["a"], 1)
                evme = P.emit("scalar", f_mev,
                              waits=[("t", evst2[0][1]), ("sq", P.ev("sq")[1])],
                              incs=[("a", 1)])

                def f_zst(eng, S, v=v, sl=sl, ch=ch, s4=s4):
                    return eng.dma_start(
                        out=zin[ch * TC + s4 * 128: ch * TC + (s4 + 1) * 128,
                                v * DC:(v + 1) * DC],
                        in_=sb["mst"][:, sl * TC:(sl + 1) * TC]).then_inc(S["sq"], 16)
                P.emit(STQ, f_zst, waits=[("a", evme[0][1])], incs=[("sq", 16)])
            P.emit("tensor", None, waits=[("a", P.ev("a")[1])])
        P.emit(LD, None, waits=[("sq", P.ev("sq")[1])])
        P.emit("vector", None, waits=[("t", P.ev("t")[1])])

    ev_zin = P.ev("sq")
    P.barrier()

    def f_rs(eng, S):
        return eng.collective_compute(
            "ReduceScatter", ALU.add, replica_groups=[list(range(R))],
            ins=[zin.tensor.ap().opt()], outs=[sout.tensor.ap().opt()],
        ).then_inc(S["c"], 1)
    ev_rs = P.emit("gpsimd", f_rs, waits=[("sq", ev_zin[1])], incs=[("c", 1)])
    sfree("mst", "state", "yac", "hb", "dbxb", "dab", "crep", "brep",
          "rrep", "mrep", "uu", "xnd", "spB", "spA", "dtc", "dtinb", "dtin")
    pfree("p_str", "p_xtr", "p_dt3", "p_dt2", "p_dt1", "p_dt0")

    for e in ENGINES:
        P.emit(e, None, waits=[("c", ev_rs[0][1])])

    alloc("quw", (128, NKT * 128), U16)
    alloc("srw", (128, NKT * 128), F16)
    alloc("nib", (128, NKT * 128), U16)
    alloc("wsg", (128, NKT * 128), BF16)
    alloc("wsu", (128, NKT * 128), BF16)
    alloc("intT", (128, TG), BF16)
    alloc("gsb", (128, TG), BF16)
    alloc("zps", (32, 256), BF16)

    palloc("p_gt", (128, TG))
    palloc("p_up", (128, TG))
    palloc("p_xgi", (NIT, TG))

    for h in (0, 1):
        if h == 1:
            def f_xld(eng, S):
                return eng.dma_start(
                    out=sb["xnT"][:].rearrange("p (k t) -> p k t", k=NKT),
                    in_=xnT_dram.rearrange("k p t -> p k t")).then_inc(S["li"], 16)
            evx2 = P.emit(LD, f_xld, waits=[("t", P.ev("t")[1])], incs=[("li", 16)])
            P.emit("tensor", None, waits=[("li", evx2[0][1])])

        for nt in range(NIT):
            for (wq, wsrep, wst, zrow, pst, which) in (
                (qg, sg, "wsg", 0, "p_gt", "g"),
                (qu, su, "wsu", 1, "p_up", "u"),
            ):
                evq = P.emit(LD, dma_to(
                    sb["quw"][:].rearrange("p (k n) -> p k n", k=NKT),
                    wq[nt]), waits=[("v", P.ev("v")[1])], incs=[("lq", 16)])
                evsr = P.emit(LD, dma_to(
                    sb["srw"][:].rearrange("p (k n) -> p k n", k=NKT),
                    wsrep[nt]), waits=[("v", P.ev("v")[1])], incs=[("lq", 16)])
                evzl = P.emit(LD, dma_to(
                    sb["zps"][0:32, 0:128],
                    zp_dram[zrow, 0:32, nt * 128:(nt + 1) * 128], q="lz"),
                    waits=[("t", P.ev("t")[1])], incs=[("lz", 16)])

                def f_dq(eng, S, wst=wst):
                    eng.tensor_scalar(out=sb["nib"][:], in0=sb["quw"][:],
                                      scalar1=sb["shiftv"][:], scalar2=15,
                                      op0=ALU.logical_shift_right, op1=ALU.bitwise_and)
                    return eng.tensor_mul(sb[wst][:], sb["nib"][:], sb["srw"][:]
                                          ).then_inc(S["v"], 1)
                evdq = P.emit("vector", f_dq,
                              waits=[("lq", evsr[0][1]), ("t", P.ev("t")[1])],
                              incs=[("v", 1)])

                def f_mm(eng, S, wst=wst, pst=pst, h=h):
                    pp = ps[pst]
                    last = None
                    for k in range(NKT):
                        for hh in range(2):
                            rr = slice(hh * 512, (hh + 1) * 512)
                            last = eng.matmul(
                                pp[:, rr],
                                sb[wst][:, k * 128:(k + 1) * 128],
                                sb["xnT"][:, k * TG:(k + 1) * TG][:, rr],
                                start=(k == 0), stop=False)
                    for hh in range(2):
                        rr = slice(hh * 512, (hh + 1) * 512)
                        last = eng.matmul(pp[:, rr], sb["zps"][0:32, 0:128],
                                          sb["xg"][0:32, h * TG:(h + 1) * TG][:, rr],
                                          start=False, stop=True)
                    return last.then_inc(S["t"], 1)
                evmm = P.emit("tensor", f_mm,
                              waits=[("v", evdq[0][1]), ("lz", evzl[0][1]),
                                     ("a" if which == "g" else "v", P.ev("a")[1] if which == "g" else P.ev("v")[1])],
                              incs=[("t", 1)])

                if which == "g":
                    def f_sil(eng, S):
                        return eng.activation(out=sb["gsb"][:], in_=ps["p_gt"][:],
                                              func=AF.Silu).then_inc(S["a"], 1)
                    evsl = P.emit("scalar", f_sil,
                                  waits=[("t", evmm[0][1]), ("v", P.ev("v")[1])],
                                  incs=[("a", 1)])
                else:
                    def f_glu(eng, S):
                        return eng.tensor_mul(sb["intT"][:], ps["p_up"][:],
                                              sb["gsb"][:]).then_inc(S["v"], 1)
                    evgl = P.emit("vector", f_glu,
                                  waits=[("t", evmm[0][1]), ("a", P.ev("a")[1]),
                                         ("sq", P.ev("sq")[1])],
                                  incs=[("v", 1)])

                    def f_xgi(eng, S, nt=nt):
                        last = None
                        for hh in range(2):
                            rr = slice(hh * 512, (hh + 1) * 512)
                            last = eng.matmul(ps["p_xgi"][:, rr],
                                              sb["indd"][:, nt * NIT:(nt + 1) * NIT],
                                              sb["intT"][:, rr],
                                              start=(nt == 0), stop=(nt == NIT - 1))
                        return last.then_inc(S["t"], 1)
                    evxgi = P.emit("tensor", f_xgi, waits=[("v", evgl[0][1])],
                                   incs=[("t", 1)])

                    def f_ist(eng, S, nt=nt, h=h):
                        return eng.dma_start(
                            out=inter_dram[nt, :, h * TG:(h + 1) * TG],
                            in_=sb["intT"][:]).then_inc(S["sq"], 16)
                    P.emit(STQ, f_ist, waits=[("v", evgl[0][1])], incs=[("sq", 16)])

        def f_xgiev(eng, S, h=h):
            return eng.tensor_scalar(out=sb["xgi"][0:NIT, h * TG:(h + 1) * TG],
                                     in0=ps["p_xgi"][:], scalar1=-1.0,
                                     scalar2=None, op0=ALU.mult).then_inc(S["v"], 1)
        evgie = P.emit("vector", f_xgiev, waits=[("t", P.ev("t")[1])], incs=[("v", 1)])
        P.emit("tensor", None, waits=[("v", evgie[0][1])])

    ev_inter = P.ev("sq")
    P.barrier()
    sfree("zps", "gsb")
    pfree("p_xgi", "p_up", "p_gt")

    for k in range(NIT):
        evq = P.emit(LD, dma_to(sb["quw"][:, 0:D], qd[k * 128:(k + 1) * 128, :]),
                     waits=[("sq", P.ev("sq")[1])], incs=[("lq", 16)])
        evs = P.emit(LD, dma_to(sb["srw"][:, 0:D], sd[k * 128:(k + 1) * 128, :]),
                     waits=[("sq", P.ev("sq")[1])], incs=[("lq", 16)])

        def f_dqd(eng, S):
            eng.tensor_scalar(out=sb["nib"][:, 0:D], in0=sb["quw"][:, 0:D],
                              scalar1=sb["shiftv"][:], scalar2=15,
                              op0=ALU.logical_shift_right, op1=ALU.bitwise_and)
            return eng.tensor_mul(sb["wsg"][:, 0:D], sb["nib"][:, 0:D],
                                  sb["srw"][:, 0:D]).then_inc(S["v"], 1)
        evdq = P.emit("vector", f_dqd,
                      waits=[("lq", evs[0][1]), ("sq", P.ev("sq")[1])],
                      incs=[("v", 1)])

        def f_wst(eng, S, k=k):
            return eng.dma_start(
                out=wd_deq[k].rearrange("d p n -> p d n"),
                in_=sb["wsg"][:, 0:D].rearrange("p (d n) -> p d n", d=NKT),
            ).then_inc(S["sq"], 16)
        P.emit(STQ, f_wst, waits=[("v", evdq[0][1])], incs=[("sq", 16)])

    ev_wd = P.ev("sq")
    P.barrier()
    sfree("intT", "wsu", "wsg", "nib", "srw", "quw", "xnT", "indd")

    alloc("interc", (128, NIT * 512), BF16)
    alloc("wdl", (128, 8 * 4 * 128), BF16)
    alloc("wdl2", (128, 8 * 4 * 128), BF16)
    alloc("g16q", (128, 4 * 512), F16)
    alloc("gsq", (128, 512), F32)
    alloc("xtl", (128, 512), F32)
    alloc("stl", (128, 512), F32)
    alloc("otl", (128, 512), F32)

    palloc("p_d0", (128, 512))
    palloc("p_d1", (128, 512))
    palloc("p_d2", (128, 512))
    palloc("p_d3", (128, 512))
    palloc("p_g", (128, 512), F16)

    P.emit(LD, None, waits=[("c", ev_rs[0][1]), ("sq", P.ev("sq")[1])])

    KG = [list(range(g * 8, min((g + 1) * 8, NIT))) for g in range((NIT + 7) // 8)]

    for q4 in range(4):
        for gi, kg in enumerate(KG):
            def f_il(eng, S, kg=kg, q4=q4):
                dst = sb["interc"][:].rearrange("p (k t) -> p k t", k=NIT)[
                    :, kg[0]:kg[-1] + 1, :]
                src = inter_dram[kg[0]:kg[-1] + 1, :, q4 * 512:(q4 + 1) * 512]\
                    .rearrange("k p t -> p k t")
                return eng.dma_start(out=dst, in_=src).then_inc(S["li"], 16)
            P.emit(LD, f_il, waits=[("t", P.ev("t")[1])] if gi == 0 else (),
                   incs=[("li", 16)])
        ev_ic = P.ev("li")

        for qd_i in range(8):
            for gi, kg in enumerate(KG):
                buf = "wdl" if gi % 2 == 0 else "wdl2"

                def f_wl(eng, S, kg=kg, qd_i=qd_i, buf=buf):
                    last = None
                    for dd in range(4):
                        dst = sb[buf][:].rearrange("p (k dd n) -> p k dd n",
                                                   k=8, dd=4)[:, 0:len(kg), dd, :]
                        srcv = wd_deq[kg[0]:kg[-1] + 1, qd_i * 4 + dd, :, :]\
                            .rearrange("k p n -> p k n")
                        last = eng.dma_start(out=dst, in_=srcv).then_inc(S["lq"], 16)
                    return last
                evwl = P.emit(LD, f_wl, waits=[("t", P.ev("t")[1])], incs=[("lq", 64)])

                def f_dmm(eng, S, kg=kg, gi=gi, buf=buf, qd_i=qd_i):
                    last = None
                    for ki, k in enumerate(kg):
                        for dd in range(4):
                            last = eng.matmul(
                                ps[f"p_d{dd}"][:],
                                sb[buf][:].rearrange("p (k dd n) -> p k dd n",
                                                     k=8, dd=4)[:, ki, dd, :],
                                sb["interc"][:, k * 512:(k + 1) * 512],
                                start=(gi == 0 and ki == 0), stop=False)
                    return last.then_inc(S["t"], 1)
                P.emit("tensor", f_dmm,
                       waits=[("lq", evwl[0][1]), ("li", ev_ic[1])],
                       incs=[("t", 1)])

            def f_zld(eng, S, qd_i=qd_i):
                return eng.dma_start(
                    out=sb["wdl"][0:NIT, 0:512],
                    in_=zp_dram[2, 0:NIT, qd_i * 512:(qd_i + 1) * 512]).then_inc(S["lz"], 16)
            evzl = P.emit(LD, f_zld, waits=[("t", P.ev("t")[1])], incs=[("lz", 16)])

            def f_dzp(eng, S, qd_i=qd_i, q4=q4):
                last = None
                for dd in range(4):
                    last = eng.matmul(
                        ps[f"p_d{dd}"][:],
                        sb["wdl"][0:NIT, dd * 128:(dd + 1) * 128],
                        sb["xgi"][0:NIT, q4 * 512:(q4 + 1) * 512],
                        start=False, stop=True)
                return last.then_inc(S["t"], 1)
            evzp = P.emit("tensor", f_dzp, waits=[("lz", evzl[0][1])], incs=[("t", 1)])

            def f_cast(eng, S):
                last = None
                for dd in range(4):
                    last = eng.activation(out=sb["g16q"][:, dd * 512:(dd + 1) * 512],
                                          in_=ps[f"p_d{dd}"][:], func=AF.Copy)
                return last.then_inc(S["a"], 1)
            evc6 = P.emit("scalar", f_cast, waits=[("t", evzp[0][1])], incs=[("a", 1)])
            P.emit("tensor", None, waits=[("a", evc6[0][1])])

            for s4 in range(4):
                def f_gtr(eng, S, s4=s4):
                    last = None
                    for dd in range(4):
                        last = eng.transpose(
                            ps["p_g"][:, dd * 128:(dd + 1) * 128],
                            sb["g16q"][:, dd * 512 + s4 * 128: dd * 512 + (s4 + 1) * 128],
                            sb["id_f16"][:])
                    return last.then_inc(S["t"], 1)
                evgt = P.emit("tensor", f_gtr, waits=[("a", evc6[0][1]),
                                                      ("v", P.ev("v")[1])],
                              incs=[("t", 1)])

                def f_gsil(eng, S):
                    return eng.activation(out=sb["gsq"][:], in_=ps["p_g"][:],
                                          func=AF.Silu).then_inc(S["a"], 1)
                evgs = P.emit("scalar", f_gsil,
                              waits=[("t", evgt[0][1]), ("v", P.ev("v")[1])],
                              incs=[("a", 1)])
                P.emit("tensor", None, waits=[("a", evgs[0][1])])

                rows = slice(q4 * 512 + s4 * 128, q4 * 512 + (s4 + 1) * 128)
                cols = slice(qd_i * 512, (qd_i + 1) * 512)
                evx3 = P.emit(LD, dma_to(sb["xtl"][:], xr[rows, cols], q="lx"),
                              waits=[("v", P.ev("v")[1])], incs=[("lx", 16)])
                evs3 = P.emit(LD, dma_to(sb["stl"][:], sout[rows, cols], q="lx"),
                              incs=[("lx", 16)])

                def f_comb(eng, S):
                    eng.tensor_mul(sb["otl"][:], sb["stl"][:], sb["gsq"][:])
                    return eng.tensor_add(sb["otl"][:], sb["otl"][:],
                                          sb["xtl"][:]).then_inc(S["v"], 1)
                evcb = P.emit("vector", f_comb,
                              waits=[("a", evgs[0][1]), ("lx", evs3[0][1]),
                                     ("sq", P.ev("sq")[1])],
                              incs=[("v", 1)])

                def f_ost(eng, S, rows=rows, cols=cols):
                    return eng.dma_start(out=o[rows, cols],
                                         in_=sb["otl"][:]).then_inc(S["sq"], 16)
                P.emit(STQ, f_ost, waits=[("v", evcb[0][1])], incs=[("sq", 16)])

    P.emit(STQ, None, waits=[("sq", P.ev("sq")[1])])
    P.barrier()

    with ExitStack() as st2:
        SEM = {}
        for s in ("lq", "sq", "v", "a", "t", "p", "c", "bar", "lz", "li", "lx"):
            SEM[s] = st2.enter_context(nc.semaphore(s))
        with nc.Block() as block:
            def mk(engname):
                def body(eng):
                    for (e, fn, waits) in P.steps:
                        if e != engname:
                            continue
                        for (s, val) in waits:
                            if val > 0:
                                eng.wait_ge(SEM[s], val)
                        if fn is not None:
                            fn(eng, SEM)
                return body
            block.sync(mk("sync"))
            block.vector(mk("vector"))
            block.scalar(mk("scalar"))
            block.tensor(mk("tensor"))
            block.gpsimd(mk("gpsimd"))

    for g in reversed(list(guards.values())):
        g.__exit__(None, None, None)
    return nc


_NC_CACHE = {}


def _host_prep(L0_attn_x, L0_attn_A_log, L0_attn_D, L0_attn_x_proj_w,
               L0_attn_dt_proj_w, L0_attn_dt_proj_b,
               gq, gs, gz, uq, us, uz, dq, ds, dz):
    x2 = np.ascontiguousarray(np.asarray(L0_attn_x).reshape(TT, D))

    def mview(qw):
        a = np.asarray(qw)
        k8, n = a.shape
        return a.view('<u2').reshape(k8, n, 2).transpose(0, 2, 1).reshape(2 * k8, n)

    def tile_gu(qw):
        rep = np.repeat(mview(qw), 4, axis=0)
        return np.ascontiguousarray(
            rep.reshape(NKT, 128, NIT, 128).transpose(2, 1, 0, 3))

    def tile_s(s):
        s = np.asarray(s)
        t = np.ascontiguousarray(
            np.broadcast_to(s.reshape(1, 32, 1, NIT, 128)
                            .transpose(0, 3, 2, 1, 4), (1, NIT, 128, 32, 128))[0])
        return t

    qg_t = tile_gu(gq)
    qu_t = tile_gu(uq)
    sg_t = tile_s(gs)
    su_t = tile_s(us)
    qd_t = np.ascontiguousarray(np.repeat(mview(dq), 4, axis=0))
    sd_t = np.ascontiguousarray(np.repeat(np.asarray(ds), 128, axis=0))
    zg_t = np.ascontiguousarray(np.asarray(gz).view('<u2').reshape(32, 2752))
    zu_t = np.ascontiguousarray(np.asarray(uz).view('<u2').reshape(32, 2752))
    zd_t = np.ascontiguousarray(np.asarray(dz).view('<u2').reshape(NIT, 1024))
    xpw_t = np.ascontiguousarray(np.asarray(L0_attn_x_proj_w).T)

    Al = np.asarray(L0_attn_A_log)
    dtwf = np.asarray(L0_attn_dt_proj_w)
    dtbf = np.asarray(L0_attn_dt_proj_b)
    Df = np.asarray(L0_attn_D)

    in_maps = []
    for c in range(R):
        dsl = slice(c * DC, (c + 1) * DC)
        m = np.zeros((128, R), np.float32)
        m[:, c] = 1.0
        in_maps.append({
            "xr": np.ascontiguousarray(x2[c * TCORE:(c + 1) * TCORE]),
            "xd": np.ascontiguousarray(x2[:, dsl]),
            "ah": np.ascontiguousarray(
                Al[dsl].reshape(4, 128, NST).transpose(1, 0, 2).reshape(128, 64)),
            "dtw": np.ascontiguousarray(dtwf[dsl].T),
            "dtb": np.ascontiguousarray(dtbf[dsl].reshape(4, 128).T),
            "dvec": np.ascontiguousarray(Df[dsl].reshape(4, 128).T),
            "xpw": xpw_t,
            "qg": qg_t, "qu": qu_t, "sg": sg_t, "su": su_t,
            "qd": qd_t, "sd": sd_t,
            "zg": zg_t, "zu": zu_t, "zd": zd_t,
            "sgr": np.asarray(gs), "sur": np.asarray(us), "sdr": np.asarray(ds),
            "msk": m,
        })
    return in_maps


def kernel(L0_attn_x, L0_attn_A_log, L0_attn_D, L0_attn_x_proj_w,
           L0_attn_dt_proj_w, L0_attn_dt_proj_b,
           L0_ffn_gate_qweight, L0_ffn_gate_scales, L0_ffn_gate_zeros,
           L0_ffn_up_qweight, L0_ffn_up_scales, L0_ffn_up_zeros,
           L0_ffn_down_qweight, L0_ffn_down_scales, L0_ffn_down_zeros):
    if "nc" not in _NC_CACHE:
        _NC_CACHE["nc"] = build_program()
    nc = _NC_CACHE["nc"]
    in_maps = _host_prep(
        L0_attn_x, L0_attn_A_log, L0_attn_D, L0_attn_x_proj_w,
        L0_attn_dt_proj_w, L0_attn_dt_proj_b,
        L0_ffn_gate_qweight, L0_ffn_gate_scales, L0_ffn_gate_zeros,
        L0_ffn_up_qweight, L0_ffn_up_scales, L0_ffn_up_zeros,
        L0_ffn_down_qweight, L0_ffn_down_scales, L0_ffn_down_zeros)
    res = run_bass_kernel_spmd(nc, in_maps, core_ids=list(range(R)))
    out = np.concatenate([np.asarray(res.results[c]["o"]) for c in range(R)], 0)
    return out.reshape(2, 8192, 4096)


# version 04 (final)
# speedup vs baseline: 1.0353x; 1.0353x over previous
import numpy as np
import ml_dtypes
from contextlib import ExitStack

import concourse.bass as bass
import concourse.mybir as mybir
from concourse.bass_utils import run_bass_kernel_spmd

F32 = mybir.dt.float32
F16 = mybir.dt.float16
BF16 = mybir.dt.bfloat16
U16 = mybir.dt.uint16
AF = mybir.ActivationFunctionType
ALU = mybir.AluOpType
AXL = mybir.AxisListType

R = 8
TT = 16384
D = 4096
TCORE = 2048
DC = 512
NST = 16
DTR = 128
ITR = 11008
NKT = 32
NIT = 86
TG = 1024
TC = 256
NCH = TT // TC
EPS = 1e-5

ENGINES = ("sync", "vector", "scalar", "tensor", "gpsimd")


class Prog:
    def __init__(self):
        self.steps = []
        self.counts = {}

    def ev(self, sem):
        return (sem, self.counts.get(sem, 0))

    def emit(self, eng, fn, waits=(), incs=()):
        out = []
        for s, n in incs:
            self.counts[s] = self.counts.get(s, 0) + n
            out.append((s, self.counts[s]))
        self.steps.append((eng, fn, tuple(waits)))
        return out

    def barrier(self):
        for e in ENGINES:
            self.emit(e, lambda eng, S: eng.nop().then_inc(S["bar"], 1),
                      incs=[("bar", 1)])
        tgt = self.counts["bar"]
        for e in ENGINES:
            self.emit(e, None, waits=[("bar", tgt)])


def build_program():
    nc = bass.Bass("TRN2", target_bir_lowering=False, debug=False)

    def ein(name, shape, dt):
        return nc.dram_tensor(name, list(shape), dt, kind="ExternalInput").ap()

    xr = ein("xr", (TCORE, D), F32)
    xd = ein("xd", (TT, DC), F32)
    ah = ein("ah", (128, 64), F32)
    dtw = ein("dtw", (DTR, DC), F32)
    dtb = ein("dtb", (128, 4), F32)
    dvec = ein("dvec", (128, 4), F32)
    xpw = ein("xpw", (D, 160), F32)
    qg = ein("qg", (NIT, 128, NKT, 128), U16)
    qu = ein("qu", (NIT, 128, NKT, 128), U16)
    sg = ein("sg", (NIT, 1, NKT * 128), F16)
    su = ein("su", (NIT, 1, NKT * 128), F16)
    qd = ein("qd", (ITR, D), U16)
    sd = ein("sd", (NIT, 1, D), F16)
    zg = ein("zg", (32, 2752), U16)
    zu = ein("zu", (32, 2752), U16)
    zd = ein("zd", (NIT, 1024), U16)
    sgr = ein("sgr", (32, ITR), F16)
    sur = ein("sur", (32, ITR), F16)
    sdr = ein("sdr", (NIT, D), F16)
    msk = ein("msk", (128, R), F32)

    o = nc.dram_tensor("o", [TCORE, D], F32, kind="ExternalOutput").ap()

    ag_in = nc.dram_tensor("ag_in", [162, TCORE], F32, kind="Internal").ap()
    ag_out = nc.dram_tensor("ag_out", [162 * R, TCORE], F32, kind="Internal").ap()
    xnT_dram = nc.dram_tensor("xnT_dram", [2, NKT, 128, TG], BF16, kind="Internal").ap()
    zin = nc.dram_tensor("zin", [TT, D], F32, kind="Internal").ap()
    sout = nc.dram_tensor("sout", [TCORE, D], F32, kind="Internal").ap()
    inter_dram = nc.dram_tensor("inter_dram", [NIT, 128, TCORE], BF16, kind="Internal").ap()
    wd_deq = nc.dram_tensor("wd_deq", [NIT, NKT, 128, 128], BF16, kind="Internal").ap()
    zp_dram = nc.dram_tensor("zp_dram", [3, 128, ITR], BF16, kind="Internal").ap()

    id_bf = nc.inline_tensor(np.eye(128).astype(ml_dtypes.bfloat16), "c_id_bf").ap()
    id_f32 = nc.inline_tensor(np.eye(128).astype(np.float32), "c_id_f32").ap()
    id_f16 = nc.inline_tensor(np.eye(128).astype(np.float16), "c_id_f16").ap()
    shiftv = nc.inline_tensor(
        (4 * (np.arange(128) % 4)).astype(np.uint16).reshape(128, 1), "c_shiftv").ap()
    sh4 = nc.inline_tensor(
        np.tile(np.array([0, 4, 8, 12], np.uint16), (128, 1)), "c_sh4").ap()
    indg_np = np.zeros((128, 32 * 32), np.float32)
    for g in range(32):
        indg_np[:, g * 32 + g] = 1.0
    indg = nc.inline_tensor(indg_np.astype(ml_dtypes.bfloat16), "c_indg").ap()
    indd_np = np.zeros((128, NIT * NIT), np.float32)
    for g in range(NIT):
        indd_np[:, g * NIT + g] = 1.0
    indd = nc.inline_tensor(indd_np.astype(ml_dtypes.bfloat16), "c_indd").ap()

    P = Prog()
    sb = {}
    ps = {}
    guards = {}

    def alloc(name, shape, dt, tname=None):
        g = nc.sbuf_tensor("sb_" + (tname or name), list(shape), dt)
        sb[name] = g.__enter__()
        guards[name] = g
        return sb[name]

    def sfree(*names):
        for n in names:
            guards.pop(n).__exit__(None, None, None)

    def palloc(name, shape, dt=F32):
        g = nc.psum_tensor("ps_" + name, list(shape), dt)
        ps[name] = g.__enter__()
        guards["#p_" + name] = g
        return ps[name]

    def pfree(*names):
        for n in names:
            guards.pop("#p_" + n).__exit__(None, None, None)

    alloc("id_bf", (128, 128), BF16)
    alloc("id_f32", (128, 128), F32)
    alloc("id_f16", (128, 128), F16)
    alloc("shiftv", (128, 1), U16)
    alloc("sh4", (128, 4), U16)
    alloc("A", (128, 64), F32)
    alloc("dtwb", (128, DC), BF16)
    alloc("dtb", (128, 4), F32)
    alloc("dvec", (128, 4), F32)
    alloc("xpwb", (128, NKT * 160), BF16)
    alloc("indg", (128, 32 * 32), BF16)
    alloc("msk", (128, R), F32)
    alloc("xg", (32, TCORE), BF16)
    alloc("xgi", (NIT, TCORE), BF16)
    alloc("indd", (128, NIT * NIT), BF16)

    def dma_to(dst, src, q="lq", eng=None):
        def fn(e, S, dst=dst, src=src, q=q):
            return e.dma_start(out=dst, in_=src).then_inc(S[q], 16)
        return fn

    LD = "sync"
    STQ = "scalar"

    for dst, src in [
        (sb["id_bf"][:], id_bf), (sb["id_f32"][:], id_f32),
        (sb["id_f16"][:], id_f16), (sb["shiftv"][:], shiftv),
        (sb["sh4"][:], sh4), (sb["dtb"][:], dtb), (sb["dvec"][:], dvec),
        (sb["indg"][:], indg), (sb["indd"][:], indd),
        (sb["msk"][:], msk),
    ]:
        P.emit(LD, dma_to(dst, src), incs=[("lq", 16)])

    alloc("c_ah", (128, 64), F32)
    alloc("c_dtw", (128, DC), F32)
    alloc("c_xpw", (128, NKT * 160), F32)
    P.emit(LD, dma_to(sb["c_ah"][:], ah), incs=[("lq", 16)])
    P.emit(LD, dma_to(sb["c_dtw"][:], dtw), incs=[("lq", 16)])
    P.emit(LD, dma_to(sb["c_xpw"][:].rearrange("p (k r) -> p k r", k=NKT),
                      xpw.rearrange("(k p) r -> p k r", p=128)), incs=[("lq", 16)])

    alloc("z_in", (NIT, 2752), U16)
    alloc("z_sh", (NIT, ITR), U16)
    alloc("z_s", (NIT, ITR), F16)
    alloc("z_out", (NIT, ITR), BF16)

    def z_extract(zsrc, grows, gcols, s_src, zp_row):
        nn = gcols * 4
        evl = P.emit(LD, dma_to(sb["z_in"][0:grows, 0:gcols], zsrc),
                     waits=[("v", P.ev("v")[1])], incs=[("lq", 16)])
        evl2 = P.emit(LD, dma_to(sb["z_s"][0:grows, 0:nn], s_src), incs=[("lq", 16)])

        def f_shift(eng, S, grows=grows, gcols=gcols, nn=nn):
            last = None
            for j in range(4):
                ov = sb["z_sh"][0:grows, 0:nn].rearrange("g (m j) -> g m j", j=4)[:, :, j]
                last = eng.tensor_scalar(out=ov, in0=sb["z_in"][0:grows, 0:gcols],
                                         scalar1=4 * j, scalar2=15,
                                         op0=ALU.logical_shift_right, op1=ALU.bitwise_and)
            return last.then_inc(S["v"], 1)
        ev1 = P.emit("vector", f_shift,
                     waits=[("lq", evl2[0][1]), ("v", P.ev("v")[1])], incs=[("v", 1)])

        def f_p1(eng, S, grows=grows, nn=nn):
            return eng.tensor_single_scalar(sb["z_sh"][0:grows, 0:nn],
                                            sb["z_sh"][0:grows, 0:nn],
                                            1, ALU.add).then_inc(S["v"], 1)
        ev2 = P.emit("vector", f_p1, waits=[("v", ev1[0][1])], incs=[("v", 1)])

        def f_zmul(eng, S, grows=grows, nn=nn):
            return eng.tensor_mul(sb["z_out"][0:grows, 0:nn], sb["z_sh"][0:grows, 0:nn],
                                  sb["z_s"][0:grows, 0:nn]).then_inc(S["v"], 1)
        evz = P.emit("vector", f_zmul,
                     waits=[("v", ev2[0][1]), ("lq", evl2[0][1]),
                            ("sq", P.ev("sq")[1])], incs=[("v", 1)])
        P.emit(STQ, dma_to(zp_dram[zp_row, 0:grows, 0:nn],
                           sb["z_out"][0:grows, 0:nn], q="sq"),
               waits=[("v", evz[0][1])], incs=[("sq", 16)])

    z_extract(zg, 32, 2752, sgr, 0)
    z_extract(zu, 32, 2752, sur, 1)
    z_extract(zd, NIT, 1024, sdr, 2)

    ev_cl = P.ev("lq")

    def f_A(eng, S):
        eng.wait_ge(S["lq"], ev_cl[1])
        eng.activation(out=sb["A"][:], in_=sb["c_ah"][:], func=AF.Exp)
        return eng.activation(out=sb["A"][:], in_=sb["A"][:], func=AF.Copy,
                              scale=-1.0).then_inc(S["a"], 1)
    P.emit("scalar", f_A, incs=[("a", 1)])

    def f_casts(eng, S):
        eng.wait_ge(S["lq"], ev_cl[1])
        eng.tensor_copy(sb["dtwb"][:], sb["c_dtw"][:])
        return eng.tensor_copy(sb["xpwb"][:], sb["c_xpw"][:]).then_inc(S["v"], 1)
    P.emit("vector", f_casts, incs=[("v", 1)])

    P.emit(STQ, None, waits=[("sq", P.ev("sq")[1])])
    P.barrier()
    sfree("z_out", "z_s", "z_sh", "z_in", "c_xpw", "c_dtw", "c_ah")

    alloc("xnT", (128, NKT * TG), BF16)
    alloc("xt", (128, D), F32)
    alloc("xsq", (128, D), F32)
    alloc("xn", (128, D), BF16)
    alloc("cols", (128, 8), F32)
    alloc("xpst", (128, TG), F32)

    palloc("p_tr", (128, 512), BF16)
    palloc("p_xp", (128, TG), F32)
    palloc("p_bc", (32, TG), F32)
    palloc("p_xg", (32, TG), F32)

    for h in (1, 0):
        for tl in range(TG // 128):
            tt = h * (TG // 128) + tl
            evx = P.emit(LD, dma_to(sb["xt"][:], xr[tt * 128:(tt + 1) * 128, :]),
                         waits=[("v", P.ev("v")[1]), ("a", P.ev("a")[1])],
                         incs=[("lq", 16)])

            def f_stats_a(eng, S):
                c = sb["cols"]
                eng.tensor_mul(sb["xsq"][:], sb["xt"][:], sb["xt"][:])
                eng.reduce_sum(c[:, 0:1], sb["xt"][:], axis=AXL.X)
                return eng.reduce_sum(c[:, 1:2], sb["xsq"][:], axis=AXL.X).then_inc(S["v"], 1)
            evsa = P.emit("vector", f_stats_a, waits=[("lq", evx[0][1])],
                          incs=[("v", 1)])

            def f_stats_s(eng, S):
                c = sb["cols"]
                eng.mul(c[:, 2:3], c[:, 0:1], 1.0 / D)
                eng.activation(out=c[:, 3:4], in_=c[:, 1:2], func=AF.Copy,
                               scale=1.0 / D, bias=EPS)
                eng.activation(out=c[:, 7:8], in_=c[:, 2:3], func=AF.Square)
                eng.activation(out=c[:, 5:6], in_=c[:, 7:8], func=AF.Ln,
                               scale=-1.0, bias=c[:, 3:4])
                eng.mul(c[:, 5:6], c[:, 5:6], -0.5)
                return eng.activation(out=c[:, 6:7], in_=c[:, 5:6],
                                      func=AF.Exp).then_inc(S["a"], 1)
            evr = P.emit("scalar", f_stats_s,
                         waits=[("v", evsa[0][1]), ("sq", P.ev("sq")[1])],
                         incs=[("a", 1)])

            def f_xn(eng, S):
                c = sb["cols"]
                return eng.tensor_scalar(out=sb["xn"][:], in0=sb["xt"][:],
                                         scalar1=c[:, 2:3], scalar2=c[:, 6:7],
                                         op0=ALU.subtract, op1=ALU.mult).then_inc(S["v"], 1)
            evn = P.emit("vector", f_xn,
                         waits=[("a", evr[0][1]), ("t", P.ev("t")[1])],
                         incs=[("v", 1)])

            def f_stst(eng, S, tt=tt):
                c = sb["cols"]
                eng.dma_start(out=ag_in[160:161, tt * 128:(tt + 1) * 128],
                              in_=c[:, 2:3]).then_inc(S["sq"], 16)
                return eng.dma_start(out=ag_in[161:162, tt * 128:(tt + 1) * 128],
                                     in_=c[:, 6:7]).then_inc(S["sq"], 16)
            P.emit(STQ, f_stst, waits=[("a", evr[0][1])], incs=[("sq", 32)])

            for j in range(8):
                def f_tp(eng, S, j=j):
                    last = None
                    for q in range(4):
                        k = j * 4 + q
                        last = eng.transpose(ps["p_tr"][:, q * 128:(q + 1) * 128],
                                             sb["xn"][:, k * 128:(k + 1) * 128],
                                             sb["id_bf"][:])
                    return last.then_inc(S["t"], 1)
                evt = P.emit("tensor", f_tp,
                             waits=[("v", evn[0][1]),
                                    ("v", P.ev("v")[1])],
                             incs=[("t", 1)])

                def f_ev(eng, S, j=j, tl=tl):
                    dst = sb["xnT"][:].rearrange("p (k t) -> p k t", k=NKT)[
                        :, j * 4:(j + 1) * 4, tl * 128:(tl + 1) * 128]
                    src = ps["p_tr"][:].rearrange("p (q t) -> p q t", q=4)
                    return eng.tensor_copy(dst, src).then_inc(S["v"], 1)
                evtc = P.emit("vector", f_ev, waits=[("t", evt[0][1])],
                              incs=[("v", 1)])
                P.emit("tensor", None, waits=[("v", evtc[0][1])])

        def f_xp(eng, S):
            last = None
            for k in range(NKT):
                st = (k == 0)
                sp = (k == NKT - 1)
                for hh in range(2):
                    rr = slice(hh * 512, (hh + 1) * 512)
                    eng.matmul(ps["p_xp"][:, rr],
                               sb["xpwb"][:, k * 160:k * 160 + 128],
                               sb["xnT"][:, k * TG:k * TG + TG][:, rr],
                               start=st, stop=sp)
                    eng.matmul(ps["p_bc"][:, rr],
                               sb["xpwb"][:, k * 160 + 128:(k + 1) * 160],
                               sb["xnT"][:, k * TG:k * TG + TG][:, rr],
                               start=st, stop=sp)
                    last = eng.matmul(ps["p_xg"][:, rr],
                                      sb["indg"][:, k * 32:(k + 1) * 32],
                                      sb["xnT"][:, k * TG:k * TG + TG][:, rr],
                                      start=st, stop=sp)
            return last.then_inc(S["t"], 1)
        evm = P.emit("tensor", f_xp, incs=[("t", 1)])

        def f_xpev(eng, S, h=h):
            eng.tensor_copy(sb["xpst"][:], ps["p_xp"][:])
            return eng.tensor_scalar(out=sb["xg"][0:32, h * TG:(h + 1) * TG],
                                     in0=ps["p_xg"][:], scalar1=-1.0,
                                     scalar2=None, op0=ALU.mult).then_inc(S["v"], 1)
        evxp = P.emit("vector", f_xpev, waits=[("t", evm[0][1])], incs=[("v", 1)])

        def f_bcev(eng, S):
            return eng.tensor_copy(sb["xpst"][0:32, 0:TG], ps["p_bc"][:]).then_inc(S["v"], 1)
        def f_st1(eng, S, h=h):
            return eng.dma_start(out=ag_in[0:128, h * TG:(h + 1) * TG],
                                 in_=sb["xpst"][:]).then_inc(S["sq"], 16)
        ev1 = P.emit(STQ, f_st1, waits=[("v", evxp[0][1])], incs=[("sq", 16)])
        evbc = P.emit("vector", f_bcev, waits=[("t", evm[0][1]), ("sq", ev1[0][1])],
                      incs=[("v", 1)])

        def f_st2(eng, S, h=h):
            return eng.dma_start(out=ag_in[128:160, h * TG:(h + 1) * TG],
                                 in_=sb["xpst"][0:32, 0:TG]).then_inc(S["sq"], 16)
        ev2 = P.emit(STQ, f_st2, waits=[("v", evbc[0][1])], incs=[("sq", 16)])
        P.emit("tensor", None, waits=[("sq", ev2[0][1])])

        def f_xstash(eng, S, h=h):
            return eng.dma_start(
                out=xnT_dram[h].rearrange("k p t -> p k t"),
                in_=sb["xnT"][:].rearrange("p (k t) -> p k t", k=NKT),
            ).then_inc(S["sq"], 16)
        evst = P.emit(STQ, f_xstash, incs=[("sq", 16)])
        P.emit("vector", None, waits=[("sq", evst[0][1])])
        P.emit("tensor", None, waits=[("sq", evst[0][1])])

    ev_agin = P.ev("sq")
    P.barrier()
    sfree("xpst", "cols", "xn", "xsq", "xt", "xnT")
    pfree("p_xg", "p_bc", "p_xp", "p_tr")

    def f_ag(eng, S):
        return eng.collective_compute(
            "AllGather", ALU.bypass, replica_groups=[list(range(R))],
            ins=[ag_in.tensor.ap().opt()], outs=[ag_out.tensor.ap().opt()],
        ).then_inc(S["c"], 1)
    ev_ag = P.emit("gpsimd", f_ag, waits=[("sq", ev_agin[1])], incs=[("c", 1)])

    alloc("dtin", (128, TC), F32)
    alloc("dtinb", (128, TC), BF16)
    alloc("dtc", (128, 4 * TC), F32)
    alloc("spA", (128, TC), F32)
    alloc("spB", (128, TC), F32)
    alloc("xnd", (128, 4 * TC), F32)
    alloc("uu", (128, TC), F32)
    alloc("mrep", (128, TC), F32)
    alloc("rrep", (128, TC), F32)
    alloc("brepA", (128, 16 * TC), F32)
    alloc("crepA", (128, 16 * TC), F32)
    alloc("adt", (128, 16 * TC), F32)
    alloc("dbx", (128, 16 * TC), F32)
    alloc("hb", (128, 16 * TC), F32)
    alloc("yac", (128, 4 * TC), F32)
    alloc("ytr", (128, 4 * 128), F32)
    alloc("ymask", (128, R * DC), F32)
    alloc("state", (128, 64), F32)
    alloc("mst", (128, DC), F32)

    palloc("p_dt0", (128, TC))
    palloc("p_dt1", (128, TC))
    palloc("p_dt2", (128, TC))
    palloc("p_dt3", (128, TC))
    palloc("p_xtr", (128, 4 * 128))
    palloc("p_str", (128, 4 * 128))

    P.emit("vector", None, waits=[("c", ev_ag[0][1])])
    P.emit("scalar", None, waits=[("c", ev_ag[0][1])])
    P.emit("tensor", None, waits=[("c", ev_ag[0][1])])
    P.emit(LD, None, waits=[("c", ev_ag[0][1])])

    NS4 = TC // 128
    for ch in range(NCH):
        blk = ch // (TCORE // TC)
        lt = (ch % (TCORE // TC)) * TC
        base = blk * 162

        evd = P.emit(LD, dma_to(sb["dtin"][:],
                                ag_out[base:base + 128, lt:lt + TC]),
                     waits=[("t", P.ev("t")[1]), ("v", P.ev("v")[1])],
                     incs=[("lq", 16)])
        evmr = P.emit(LD, dma_to(
            sb["mrep"][:], ag_out[base + 160:base + 161, lt:lt + TC].partition_broadcast(128)),
            incs=[("lq", 16)])
        evrr = P.emit(LD, dma_to(
            sb["rrep"][:], ag_out[base + 161:base + 162, lt:lt + TC].partition_broadcast(128)),
            incs=[("lq", 16)])
        evbb = P.emit(LD, dma_to(
            sb["brepA"][:].rearrange("p (n t) -> p n t", n=NST),
            ag_out[base + 128:base + 144, lt:lt + TC].unsqueeze(0).partition_broadcast(128)),
            incs=[("lq", 16)])
        evcc = P.emit(LD, dma_to(
            sb["crepA"][:].rearrange("p (n t) -> p n t", n=NST),
            ag_out[base + 144:base + 160, lt:lt + TC].unsqueeze(0).partition_broadcast(128)),
            incs=[("lq", 16)])

        evcast = P.emit("vector",
                        lambda eng, S: eng.tensor_copy(sb["dtinb"][:], sb["dtin"][:]).then_inc(S["v"], 1),
                        waits=[("lq", evcc[0][1])], incs=[("v", 1)])

        def f_dtmm(eng, S):
            last = None
            for i in range(4):
                last = eng.matmul(ps[f"p_dt{i}"][:], sb["dtwb"][:, i * 128:(i + 1) * 128],
                                  sb["dtinb"][:], start=True, stop=True)
            return last.then_inc(S["t"], 1)
        evdt = P.emit("tensor", f_dtmm,
                      waits=[("v", evcast[0][1]), ("a", P.ev("a")[1]),
                             ("v", P.ev("v")[1])],
                      incs=[("t", 1)])

        for i in range(4):
            def f_sp_a(eng, S, i=i):
                eng.activation(out=sb["spA"][:], in_=ps[f"p_dt{i}"][:], func=AF.Abs,
                               bias=sb["dtb"][:, i:i + 1])
                eng.activation(out=sb["spB"][:], in_=sb["spA"][:], func=AF.Exp, scale=-1.0)
                return eng.activation(out=sb["spA"][:], in_=sb["spB"][:], func=AF.Ln,
                                      bias=1.0).then_inc(S["a"], 1)
            eva = P.emit("scalar", f_sp_a,
                         waits=[("t", evdt[0][1]), ("v", P.ev("v")[1])],
                         incs=[("a", 1)])

            def f_sp_v(eng, S, i=i):
                eng.tensor_scalar(out=sb["dtc"][:, i * TC:(i + 1) * TC],
                                  in0=ps[f"p_dt{i}"][:], scalar1=sb["dtb"][:, i:i + 1],
                                  scalar2=0.0, op0=ALU.add, op1=ALU.max)
                return eng.tensor_add(sb["dtc"][:, i * TC:(i + 1) * TC],
                                      sb["dtc"][:, i * TC:(i + 1) * TC],
                                      sb["spA"][:]).then_inc(S["v"], 1)
            evv = P.emit("vector", f_sp_v, waits=[("a", eva[0][1])], incs=[("v", 1)])
        ev_dtdone = P.ev("v")

        for s4 in range(NS4):
            def f_xl(eng, S, s4=s4, ch=ch):
                return eng.dma_start(
                    out=sb["mst"][:, 0:DC],
                    in_=xd[ch * TC + s4 * 128: ch * TC + (s4 + 1) * 128, :],
                ).then_inc(S["lq"], 16)
            evxl = P.emit(LD, f_xl, waits=[("t", P.ev("t")[1])], incs=[("lq", 16)])

            def f_xtr(eng, S, s4=s4):
                last = None
                for i in range(4):
                    last = eng.transpose(ps["p_xtr"][:, i * 128:(i + 1) * 128],
                                         sb["mst"][:, i * 128:(i + 1) * 128],
                                         sb["id_f32"][:])
                return last.then_inc(S["t"], 1)
            evxt = P.emit("tensor", f_xtr, waits=[("lq", evxl[0][1])], incs=[("t", 1)])

            def f_xev(eng, S, s4=s4):
                dst = sb["xnd"][:].rearrange("p (i t) -> p i t", i=4)[
                    :, :, s4 * 128:(s4 + 1) * 128]
                src = ps["p_xtr"][:].rearrange("p (i t) -> p i t", i=4)
                mr = sb["mrep"][:, s4 * 128:(s4 + 1) * 128].unsqueeze(1).broadcast_to([128, 4, 128])
                rr2 = sb["rrep"][:, s4 * 128:(s4 + 1) * 128].unsqueeze(1).broadcast_to([128, 4, 128])
                eng.tensor_sub(dst, src, mr)
                return eng.tensor_mul(dst, dst, rr2).then_inc(S["v"], 1)
            evxe = P.emit("vector", f_xev,
                          waits=[("t", evxt[0][1]), ("lq", evcc[0][1])], incs=[("v", 1)])
            P.emit("tensor", None, waits=[("v", evxe[0][1])])
        ev_xnd = P.ev("v")

        for i in range(4):
            nsl = slice(i * 16, (i + 1) * 16)

            def f_adt(eng, S, i=i, nsl=nsl):
                ov = sb["adt"][:].rearrange("p (n t) -> p n t", n=NST)
                in0 = sb["A"][:, nsl].unsqueeze(2).broadcast_to([128, NST, TC])
                in1 = sb["dtc"][:, i * TC:(i + 1) * TC].unsqueeze(1).broadcast_to([128, NST, TC])
                return eng.tensor_tensor(ov, in0, in1, ALU.mult).then_inc(S["v"], 1)
            ev_adt = P.emit("vector", f_adt,
                            waits=[("v", P.ev("v")[1]), ("a", P.ev("a")[1])],
                            incs=[("v", 1)])

            def f_exp(eng, S):
                return eng.activation(out=sb["adt"][:], in_=sb["adt"][:],
                                      func=AF.Exp).then_inc(S["a"], 1)
            ev_exp = P.emit("scalar", f_exp, waits=[("v", ev_adt[0][1])], incs=[("a", 1)])

            def f_u(eng, S, i=i):
                return eng.tensor_mul(sb["uu"][:], sb["dtc"][:, i * TC:(i + 1) * TC],
                                      sb["xnd"][:, i * TC:(i + 1) * TC]).then_inc(S["v"], 1)
            ev_u = P.emit("vector", f_u, waits=[("v", ev_xnd[1]), ("v", ev_dtdone[1])],
                          incs=[("v", 1)])

            def f_dbx(eng, S):
                ov = sb["dbx"][:].rearrange("p (n t) -> p n t", n=NST)
                in0 = sb["brepA"][:].rearrange("p (n t) -> p n t", n=NST)
                in1 = sb["uu"][:].unsqueeze(1).broadcast_to([128, NST, TC])
                return eng.tensor_tensor(ov, in0, in1, ALU.mult).then_inc(S["v"], 1)
            ev_dbx = P.emit("vector", f_dbx,
                            waits=[("v", ev_u[0][1]), ("lq", evbb[0][1])],
                            incs=[("v", 1)])

            def f_scans(eng, S, i=i, ch=ch):
                last = None
                for n in range(NST):
                    col = i * 16 + n
                    if ch % (8192 // TC) == 0:
                        ini = 0.0
                    else:
                        ini = sb["state"][:, col:col + 1]
                    last = eng.tensor_tensor_scan(
                        out=sb["hb"][:, n * TC:(n + 1) * TC],
                        data0=sb["adt"][:, n * TC:(n + 1) * TC],
                        data1=sb["dbx"][:, n * TC:(n + 1) * TC], initial=ini,
                        op0=ALU.mult, op1=ALU.add)
                return last.then_inc(S["v"], 1)
            ev_sc = P.emit("vector", f_scans,
                           waits=[("a", ev_exp[0][1]), ("v", ev_dbx[0][1])],
                           incs=[("v", 1)])

            def f_stcp(eng, S, i=i):
                hsrc = sb["hb"][:].rearrange("p (n t) -> p n t", n=NST)[:, :, TC - 1:TC]
                return eng.tensor_copy(sb["state"][:, i * 16:(i + 1) * 16].unsqueeze(2),
                                       hsrc).then_inc(S["v"], 1)
            ev_st = P.emit("vector", f_stcp, waits=[("v", ev_sc[0][1])], incs=[("v", 1)])

            def f_hc(eng, S):
                return eng.tensor_mul(sb["dbx"][:], sb["hb"][:],
                                      sb["crepA"][:]).then_inc(S["v"], 1)
            ev_hc = P.emit("vector", f_hc,
                           waits=[("v", ev_st[0][1]), ("lq", evcc[0][1])],
                           incs=[("v", 1)])

            def f_t1(eng, S):
                return eng.tensor_add(sb["adt"][:, 0:8 * TC], sb["dbx"][:, 0:8 * TC],
                                      sb["dbx"][:, 8 * TC:16 * TC]).then_inc(S["v"], 1)
            ev_t1 = P.emit("vector", f_t1, waits=[("v", ev_hc[0][1])], incs=[("v", 1)])

            def f_t2(eng, S):
                return eng.tensor_add(sb["adt"][:, 0:4 * TC], sb["adt"][:, 0:4 * TC],
                                      sb["adt"][:, 4 * TC:8 * TC]).then_inc(S["v"], 1)
            ev_t2 = P.emit("vector", f_t2, waits=[("v", ev_t1[0][1])], incs=[("v", 1)])

            def f_t3(eng, S):
                return eng.tensor_add(sb["adt"][:, 0:2 * TC], sb["adt"][:, 0:2 * TC],
                                      sb["adt"][:, 2 * TC:4 * TC]).then_inc(S["v"], 1)
            ev_t3 = P.emit("vector", f_t3, waits=[("v", ev_t2[0][1])], incs=[("v", 1)])

            def f_t4(eng, S, i=i):
                return eng.tensor_add(sb["yac"][:, i * TC:(i + 1) * TC],
                                      sb["adt"][:, 0:TC],
                                      sb["adt"][:, TC:2 * TC]).then_inc(S["v"], 1)
            ev_t4 = P.emit("vector", f_t4, waits=[("v", ev_t3[0][1])], incs=[("v", 1)])

            def f_dterm(eng, S, i=i):
                return eng.scalar_tensor_tensor(
                    out=sb["yac"][:, i * TC:(i + 1) * TC],
                    in0=sb["xnd"][:, i * TC:(i + 1) * TC],
                    scalar=sb["dvec"][:, i:i + 1],
                    in1=sb["yac"][:, i * TC:(i + 1) * TC],
                    op0=ALU.mult, op1=ALU.add).then_inc(S["v"], 1)
            P.emit("vector", f_dterm, waits=[("v", ev_t4[0][1])], incs=[("v", 1)])

        for s4 in range(NS4):
            def f_str(eng, S, s4=s4):
                last = None
                for i in range(4):
                    last = eng.transpose(
                        ps["p_str"][:, i * 128:(i + 1) * 128],
                        sb["yac"][:, i * TC + s4 * 128:i * TC + (s4 + 1) * 128],
                        sb["id_f32"][:])
                return last.then_inc(S["t"], 1)
            evst2 = P.emit("tensor", f_str, waits=[("v", P.ev("v")[1])], incs=[("t", 1)])

            def f_yev(eng, S):
                return eng.tensor_copy(sb["ytr"][:], ps["p_str"][:]).then_inc(S["v"], 1)
            evye = P.emit("vector", f_yev,
                          waits=[("t", evst2[0][1]), ("sq", P.ev("sq")[1])],
                          incs=[("v", 1)])

            def f_msk(eng, S):
                ov = sb["ymask"][:].rearrange("p (v d) -> p v d", v=R)
                in0 = sb["ytr"][:].unsqueeze(1).broadcast_to([128, R, DC])
                in1 = sb["msk"][:].unsqueeze(2).broadcast_to([128, R, DC])
                return eng.tensor_tensor(ov, in0, in1, ALU.mult).then_inc(S["v"], 1)
            evmk = P.emit("vector", f_msk, waits=[("v", evye[0][1])], incs=[("v", 1)])

            def f_zst(eng, S, ch=ch, s4=s4):
                last = None
                for v in range(R):
                    last = eng.dma_start(
                        out=zin[ch * TC + s4 * 128: ch * TC + (s4 + 1) * 128,
                                v * DC:(v + 1) * DC],
                        in_=sb["ymask"][:, v * DC:(v + 1) * DC]).then_inc(S["sq"], 16)
                return last
            P.emit(STQ, f_zst, waits=[("v", evmk[0][1])], incs=[("sq", 16 * R)])
            P.emit("tensor", None, waits=[("v", evmk[0][1])])
        P.emit(LD, None, waits=[("sq", P.ev("sq")[1])])
        P.emit("vector", None, waits=[("t", P.ev("t")[1])])

    ev_zin = P.ev("sq")
    P.barrier()

    def f_rs(eng, S):
        return eng.collective_compute(
            "ReduceScatter", ALU.add, replica_groups=[list(range(R))],
            ins=[zin.tensor.ap().opt()], outs=[sout.tensor.ap().opt()],
        ).then_inc(S["c"], 1)
    ev_rs = P.emit("gpsimd", f_rs, waits=[("sq", ev_zin[1])], incs=[("c", 1)])
    sfree("mst", "state", "ymask", "ytr", "yac", "hb", "dbx", "adt",
          "crepA", "brepA", "rrep", "mrep", "uu", "xnd", "spB", "spA",
          "dtc", "dtinb", "dtin")
    pfree("p_str", "p_xtr", "p_dt3", "p_dt2", "p_dt1", "p_dt0")

    for e in ENGINES:
        P.emit(e, None, waits=[("c", ev_rs[0][1])])

    alloc("xnT", (128, NKT * TG), BF16, tname="xnT5")
    alloc("quw", (128, NKT * 128), U16)
    alloc("srw", (128, NKT * 128), F16)
    alloc("nib", (128, NKT * 128), U16)
    alloc("wsg", (128, NKT * 128), BF16)
    alloc("wsu", (128, NKT * 128), BF16)
    alloc("wsg2", (128, NKT * 128), BF16)
    alloc("wsu2", (128, NKT * 128), BF16)
    alloc("intT", (128, TG), BF16)
    alloc("gsb", (128, TG), BF16)
    alloc("zps", (32, 256), BF16)

    palloc("p_gt", (128, TG))
    palloc("p_up", (128, TG))
    palloc("p_xgi", (NIT, TG))

    mm_hist = {}
    ev_dq_prev = 0
    for h in (0, 1):
        def f_xld(eng, S, h=h):
            return eng.dma_start(
                out=sb["xnT"][:].rearrange("p (k t) -> p k t", k=NKT),
                in_=xnT_dram[h].rearrange("k p t -> p k t")).then_inc(S["li"], 16)
        evx2 = P.emit(LD, f_xld, waits=[("t", P.ev("t")[1])], incs=[("li", 16)])
        P.emit("tensor", None, waits=[("li", evx2[0][1])])

        for nt in range(NIT):
            par = nt % 2
            for (wq, wsrep, wbase, zrow, pst, which) in (
                (qg, sg, "wsg", 0, "p_gt", "g"),
                (qu, su, "wsu", 1, "p_up", "u"),
            ):
                wst = wbase if par == 0 else wbase + "2"
                evq = P.emit(LD, dma_to(
                    sb["quw"][:].rearrange("p (k n) -> p k n", k=NKT),
                    wq[nt]), waits=[("v", ev_dq_prev)], incs=[("lq", 16)])
                evsr = P.emit(LD, dma_to(
                    sb["srw"][:], wsrep[nt].partition_broadcast(128)),
                    waits=[("v", ev_dq_prev)], incs=[("lq", 16)])
                evzl = P.emit(LD, dma_to(
                    sb["zps"][0:32, 0:128],
                    zp_dram[zrow, 0:32, nt * 128:(nt + 1) * 128], q="lz"),
                    waits=[("t", P.ev("t")[1])], incs=[("lz", 16)])

                def f_dq(eng, S, wst=wst):
                    eng.tensor_scalar(out=sb["nib"][:], in0=sb["quw"][:],
                                      scalar1=sb["shiftv"][:], scalar2=15,
                                      op0=ALU.logical_shift_right, op1=ALU.bitwise_and)
                    return eng.tensor_mul(sb[wst][:], sb["nib"][:], sb["srw"][:]
                                          ).then_inc(S["v"], 1)
                evdq = P.emit("vector", f_dq,
                              waits=[("lq", evsr[0][1]),
                                     ("t", mm_hist.get((which, par), 0))],
                              incs=[("v", 1)])
                ev_dq_prev = evdq[0][1]

                def f_mm(eng, S, wst=wst, pst=pst, h=h):
                    pp = ps[pst]
                    last = None
                    for k in range(NKT):
                        for hh in range(2):
                            rr = slice(hh * 512, (hh + 1) * 512)
                            last = eng.matmul(
                                pp[:, rr],
                                sb[wst][:, k * 128:(k + 1) * 128],
                                sb["xnT"][:, k * TG:(k + 1) * TG][:, rr],
                                start=(k == 0), stop=False)
                    for hh in range(2):
                        rr = slice(hh * 512, (hh + 1) * 512)
                        last = eng.matmul(pp[:, rr], sb["zps"][0:32, 0:128],
                                          sb["xg"][0:32, h * TG:(h + 1) * TG][:, rr],
                                          start=False, stop=True)
                    return last.then_inc(S["t"], 1)
                evmm = P.emit("tensor", f_mm,
                              waits=[("v", evdq[0][1]), ("lz", evzl[0][1]),
                                     ("a" if which == "g" else "v", P.ev("a")[1] if which == "g" else P.ev("v")[1])],
                              incs=[("t", 1)])
                mm_hist[(which, par)] = evmm[0][1]

                if which == "g":
                    def f_sil(eng, S):
                        return eng.activation(out=sb["gsb"][:], in_=ps["p_gt"][:],
                                              func=AF.Silu).then_inc(S["a"], 1)
                    evsl = P.emit("scalar", f_sil,
                                  waits=[("t", evmm[0][1]), ("v", P.ev("v")[1])],
                                  incs=[("a", 1)])
                else:
                    def f_glu(eng, S):
                        return eng.tensor_mul(sb["intT"][:], ps["p_up"][:],
                                              sb["gsb"][:]).then_inc(S["v"], 1)
                    evgl = P.emit("vector", f_glu,
                                  waits=[("t", evmm[0][1]), ("a", P.ev("a")[1]),
                                         ("sq", P.ev("sq")[1])],
                                  incs=[("v", 1)])

                    def f_xgi(eng, S, nt=nt):
                        last = None
                        for hh in range(2):
                            rr = slice(hh * 512, (hh + 1) * 512)
                            last = eng.matmul(ps["p_xgi"][:, rr],
                                              sb["indd"][:, nt * NIT:(nt + 1) * NIT],
                                              sb["intT"][:, rr],
                                              start=(nt == 0), stop=(nt == NIT - 1))
                        return last.then_inc(S["t"], 1)
                    evxgi = P.emit("tensor", f_xgi, waits=[("v", evgl[0][1])],
                                   incs=[("t", 1)])

                    def f_ist(eng, S, nt=nt, h=h):
                        return eng.dma_start(
                            out=inter_dram[nt, :, h * TG:(h + 1) * TG],
                            in_=sb["intT"][:]).then_inc(S["sq"], 16)
                    P.emit(STQ, f_ist, waits=[("v", evgl[0][1])], incs=[("sq", 16)])

        def f_xgiev(eng, S, h=h):
            return eng.tensor_scalar(out=sb["xgi"][0:NIT, h * TG:(h + 1) * TG],
                                     in0=ps["p_xgi"][:], scalar1=-1.0,
                                     scalar2=None, op0=ALU.mult).then_inc(S["v"], 1)
        evgie = P.emit("vector", f_xgiev, waits=[("t", P.ev("t")[1])], incs=[("v", 1)])
        P.emit("tensor", None, waits=[("v", evgie[0][1])])

    ev_inter = P.ev("sq")
    P.barrier()
    sfree("zps", "gsb")
    pfree("p_xgi", "p_up", "p_gt")

    for k in range(NIT):
        evq = P.emit(LD, dma_to(sb["quw"][:, 0:D], qd[k * 128:(k + 1) * 128, :]),
                     waits=[("sq", P.ev("sq")[1])], incs=[("lq", 16)])
        evs = P.emit(LD, dma_to(sb["srw"][:, 0:D], sd[k].partition_broadcast(128)),
                     waits=[("sq", P.ev("sq")[1])], incs=[("lq", 16)])

        def f_dqd(eng, S):
            eng.tensor_scalar(out=sb["nib"][:, 0:D], in0=sb["quw"][:, 0:D],
                              scalar1=sb["shiftv"][:], scalar2=15,
                              op0=ALU.logical_shift_right, op1=ALU.bitwise_and)
            return eng.tensor_mul(sb["wsg"][:, 0:D], sb["nib"][:, 0:D],
                                  sb["srw"][:, 0:D]).then_inc(S["v"], 1)
        evdq = P.emit("vector", f_dqd,
                      waits=[("lq", evs[0][1]), ("sq", P.ev("sq")[1])],
                      incs=[("v", 1)])

        def f_wst(eng, S, k=k):
            return eng.dma_start(
                out=wd_deq[k].rearrange("d p n -> p d n"),
                in_=sb["wsg"][:, 0:D].rearrange("p (d n) -> p d n", d=NKT),
            ).then_inc(S["sq"], 16)
        P.emit(STQ, f_wst, waits=[("v", evdq[0][1])], incs=[("sq", 16)])

    ev_wd = P.ev("sq")
    P.barrier()
    sfree("intT", "wsu2", "wsg2", "wsu", "wsg", "nib", "srw", "quw", "xnT")

    alloc("interc", (128, NIT * 512), BF16)
    alloc("wdl", (128, 8 * 4 * 128), BF16)
    alloc("wdl2", (128, 8 * 4 * 128), BF16)
    alloc("g16q", (128, 4 * 512), F16)
    alloc("gsq", (128, 512), F32)
    alloc("xtl", (128, 512), F32)
    alloc("stl", (128, 512), F32)
    alloc("otl", (128, 512), F32)

    palloc("p_d0", (128, 512))
    palloc("p_d1", (128, 512))
    palloc("p_d2", (128, 512))
    palloc("p_d3", (128, 512))
    palloc("p_g", (128, 512), F16)

    P.emit(LD, None, waits=[("c", ev_rs[0][1]), ("sq", P.ev("sq")[1])])

    KG = [list(range(g * 8, min((g + 1) * 8, NIT))) for g in range((NIT + 7) // 8)]

    for q4 in range(4):
        for gi, kg in enumerate(KG):
            def f_il(eng, S, kg=kg, q4=q4):
                dst = sb["interc"][:].rearrange("p (k t) -> p k t", k=NIT)[
                    :, kg[0]:kg[-1] + 1, :]
                src = inter_dram[kg[0]:kg[-1] + 1, :, q4 * 512:(q4 + 1) * 512]\
                    .rearrange("k p t -> p k t")
                return eng.dma_start(out=dst, in_=src).then_inc(S["li"], 16)
            P.emit(LD, f_il, waits=[("t", P.ev("t")[1])] if gi == 0 else (),
                   incs=[("li", 16)])
        ev_ic = P.ev("li")

        for qd_i in range(8):
            for gi, kg in enumerate(KG):
                buf = "wdl" if gi % 2 == 0 else "wdl2"

                def f_wl(eng, S, kg=kg, qd_i=qd_i, buf=buf):
                    last = None
                    for dd in range(4):
                        dst = sb[buf][:].rearrange("p (k dd n) -> p k dd n",
                                                   k=8, dd=4)[:, 0:len(kg), dd, :]
                        srcv = wd_deq[kg[0]:kg[-1] + 1, qd_i * 4 + dd, :, :]\
                            .rearrange("k p n -> p k n")
                        last = eng.dma_start(out=dst, in_=srcv).then_inc(S["lq"], 16)
                    return last
                evwl = P.emit(LD, f_wl, waits=[("t", P.ev("t")[1])], incs=[("lq", 64)])

                def f_dmm(eng, S, kg=kg, gi=gi, buf=buf, qd_i=qd_i):
                    last = None
                    for ki, k in enumerate(kg):
                        for dd in range(4):
                            last = eng.matmul(
                                ps[f"p_d{dd}"][:],
                                sb[buf][:].rearrange("p (k dd n) -> p k dd n",
                                                     k=8, dd=4)[:, ki, dd, :],
                                sb["interc"][:, k * 512:(k + 1) * 512],
                                start=(gi == 0 and ki == 0), stop=False)
                    return last.then_inc(S["t"], 1)
                P.emit("tensor", f_dmm,
                       waits=[("lq", evwl[0][1]), ("li", ev_ic[1])],
                       incs=[("t", 1)])

            def f_zld(eng, S, qd_i=qd_i):
                return eng.dma_start(
                    out=sb["wdl"][0:NIT, 0:512],
                    in_=zp_dram[2, 0:NIT, qd_i * 512:(qd_i + 1) * 512]).then_inc(S["lz"], 16)
            evzl = P.emit(LD, f_zld, waits=[("t", P.ev("t")[1])], incs=[("lz", 16)])

            def f_dzp(eng, S, qd_i=qd_i, q4=q4):
                last = None
                for dd in range(4):
                    last = eng.matmul(
                        ps[f"p_d{dd}"][:],
                        sb["wdl"][0:NIT, dd * 128:(dd + 1) * 128],
                        sb["xgi"][0:NIT, q4 * 512:(q4 + 1) * 512],
                        start=False, stop=True)
                return last.then_inc(S["t"], 1)
            evzp = P.emit("tensor", f_dzp, waits=[("lz", evzl[0][1])], incs=[("t", 1)])

            def f_cast(eng, S):
                last = None
                for dd in range(4):
                    last = eng.activation(out=sb["g16q"][:, dd * 512:(dd + 1) * 512],
                                          in_=ps[f"p_d{dd}"][:], func=AF.Copy)
                return last.then_inc(S["a"], 1)
            evc6 = P.emit("scalar", f_cast, waits=[("t", evzp[0][1])], incs=[("a", 1)])
            P.emit("tensor", None, waits=[("a", evc6[0][1])])

            for s4 in range(4):
                def f_gtr(eng, S, s4=s4):
                    last = None
                    for dd in range(4):
                        last = eng.transpose(
                            ps["p_g"][:, dd * 128:(dd + 1) * 128],
                            sb["g16q"][:, dd * 512 + s4 * 128: dd * 512 + (s4 + 1) * 128],
                            sb["id_f16"][:])
                    return last.then_inc(S["t"], 1)
                evgt = P.emit("tensor", f_gtr, waits=[("a", evc6[0][1]),
                                                      ("v", P.ev("v")[1])],
                              incs=[("t", 1)])

                def f_gsil(eng, S):
                    return eng.activation(out=sb["gsq"][:], in_=ps["p_g"][:],
                                          func=AF.Silu).then_inc(S["a"], 1)
                evgs = P.emit("scalar", f_gsil,
                              waits=[("t", evgt[0][1]), ("v", P.ev("v")[1])],
                              incs=[("a", 1)])
                P.emit("tensor", None, waits=[("a", evgs[0][1])])

                rows = slice(q4 * 512 + s4 * 128, q4 * 512 + (s4 + 1) * 128)
                cols = slice(qd_i * 512, (qd_i + 1) * 512)
                evx3 = P.emit(LD, dma_to(sb["xtl"][:], xr[rows, cols], q="lx"),
                              waits=[("v", P.ev("v")[1])], incs=[("lx", 16)])
                evs3 = P.emit(LD, dma_to(sb["stl"][:], sout[rows, cols], q="lx"),
                              incs=[("lx", 16)])

                def f_comb(eng, S):
                    eng.tensor_mul(sb["otl"][:], sb["stl"][:], sb["gsq"][:])
                    return eng.tensor_add(sb["otl"][:], sb["otl"][:],
                                          sb["xtl"][:]).then_inc(S["v"], 1)
                evcb = P.emit("vector", f_comb,
                              waits=[("a", evgs[0][1]), ("lx", evs3[0][1]),
                                     ("sq", P.ev("sq")[1])],
                              incs=[("v", 1)])

                def f_ost(eng, S, rows=rows, cols=cols):
                    return eng.dma_start(out=o[rows, cols],
                                         in_=sb["otl"][:]).then_inc(S["sq"], 16)
                P.emit(STQ, f_ost, waits=[("v", evcb[0][1])], incs=[("sq", 16)])

    P.emit(STQ, None, waits=[("sq", P.ev("sq")[1])])
    P.barrier()

    with ExitStack() as st2:
        SEM = {}
        for s in ("lq", "sq", "v", "a", "t", "p", "c", "bar", "lz", "li", "lx"):
            SEM[s] = st2.enter_context(nc.semaphore(s))
        with nc.Block() as block:
            def mk(engname):
                def body(eng):
                    for (e, fn, waits) in P.steps:
                        if e != engname:
                            continue
                        for (s, val) in waits:
                            if val > 0:
                                eng.wait_ge(SEM[s], val)
                        if fn is not None:
                            fn(eng, SEM)
                return body
            block.sync(mk("sync"))
            block.vector(mk("vector"))
            block.scalar(mk("scalar"))
            block.tensor(mk("tensor"))
            block.gpsimd(mk("gpsimd"))

    for g in reversed(list(guards.values())):
        g.__exit__(None, None, None)
    return nc


_NC_CACHE = {}


def _host_prep(L0_attn_x, L0_attn_A_log, L0_attn_D, L0_attn_x_proj_w,
               L0_attn_dt_proj_w, L0_attn_dt_proj_b,
               gq, gs, gz, uq, us, uz, dq, ds, dz):
    x2 = np.ascontiguousarray(np.asarray(L0_attn_x).reshape(TT, D))

    def mview(qw):
        a = np.asarray(qw)
        k8, n = a.shape
        return a.view('<u2').reshape(k8, n, 2).transpose(0, 2, 1).reshape(2 * k8, n)

    def tile_gu(qw):
        rep = np.repeat(mview(qw), 4, axis=0)
        return np.ascontiguousarray(
            rep.reshape(NKT, 128, NIT, 128).transpose(2, 1, 0, 3))

    def flat_s(s):
        s = np.asarray(s)
        return np.ascontiguousarray(
            s.reshape(32, NIT, 128).transpose(1, 0, 2).reshape(NIT, 1, NKT * 128))

    qg_t = tile_gu(gq)
    qu_t = tile_gu(uq)
    sg_t = flat_s(gs)
    su_t = flat_s(us)
    qd_t = np.ascontiguousarray(np.repeat(mview(dq), 4, axis=0))
    sd_t = np.ascontiguousarray(np.asarray(ds).reshape(NIT, 1, D))
    zg_t = np.ascontiguousarray(np.asarray(gz).view('<u2').reshape(32, 2752))
    zu_t = np.ascontiguousarray(np.asarray(uz).view('<u2').reshape(32, 2752))
    zd_t = np.ascontiguousarray(np.asarray(dz).view('<u2').reshape(NIT, 1024))
    xpw_t = np.ascontiguousarray(np.asarray(L0_attn_x_proj_w).T)

    Al = np.asarray(L0_attn_A_log)
    dtwf = np.asarray(L0_attn_dt_proj_w)
    dtbf = np.asarray(L0_attn_dt_proj_b)
    Df = np.asarray(L0_attn_D)

    in_maps = []
    for c in range(R):
        dsl = slice(c * DC, (c + 1) * DC)
        m = np.zeros((128, R), np.float32)
        m[:, c] = 1.0
        in_maps.append({
            "xr": np.ascontiguousarray(x2[c * TCORE:(c + 1) * TCORE]),
            "xd": np.ascontiguousarray(x2[:, dsl]),
            "ah": np.ascontiguousarray(
                Al[dsl].reshape(4, 128, NST).transpose(1, 0, 2).reshape(128, 64)),
            "dtw": np.ascontiguousarray(dtwf[dsl].T),
            "dtb": np.ascontiguousarray(dtbf[dsl].reshape(4, 128).T),
            "dvec": np.ascontiguousarray(Df[dsl].reshape(4, 128).T),
            "xpw": xpw_t,
            "qg": qg_t, "qu": qu_t, "sg": sg_t, "su": su_t,
            "qd": qd_t, "sd": sd_t,
            "zg": zg_t, "zu": zu_t, "zd": zd_t,
            "sgr": np.asarray(gs), "sur": np.asarray(us), "sdr": np.asarray(ds),
            "msk": m,
        })
    return in_maps


def kernel(L0_attn_x, L0_attn_A_log, L0_attn_D, L0_attn_x_proj_w,
           L0_attn_dt_proj_w, L0_attn_dt_proj_b,
           L0_ffn_gate_qweight, L0_ffn_gate_scales, L0_ffn_gate_zeros,
           L0_ffn_up_qweight, L0_ffn_up_scales, L0_ffn_up_zeros,
           L0_ffn_down_qweight, L0_ffn_down_scales, L0_ffn_down_zeros):
    if "nc" not in _NC_CACHE:
        _NC_CACHE["nc"] = build_program()
    nc = _NC_CACHE["nc"]
    in_maps = _host_prep(
        L0_attn_x, L0_attn_A_log, L0_attn_D, L0_attn_x_proj_w,
        L0_attn_dt_proj_w, L0_attn_dt_proj_b,
        L0_ffn_gate_qweight, L0_ffn_gate_scales, L0_ffn_gate_zeros,
        L0_ffn_up_qweight, L0_ffn_up_scales, L0_ffn_up_zeros,
        L0_ffn_down_qweight, L0_ffn_down_scales, L0_ffn_down_zeros)
    res = run_bass_kernel_spmd(nc, in_maps, core_ids=list(range(R)))
    out = np.concatenate([np.asarray(res.results[c]["o"]) for c in range(R)], 0)
    return out.reshape(2, 8192, 4096)
